# version 05
# baseline (speedup 1.0000x reference)
import numpy as np
import concourse.bass as bass
import concourse.mybir as mybir
from concourse.bass_utils import run_bass_kernel_spmd

F32 = mybir.dt.float32
BF16 = mybir.dt.bfloat16
AF = mybir.ActivationFunctionType
ALU = mybir.AluOpType

P = 128
D = 2048
DI = 4096
HF = 8192
NS = 16
MAGIC = 12582912.0
LN_EPS = 1e-5
RMS_EPS = 1.1920929e-07

ENGS = ("sy", "sc", "ve", "te", "po")
_DTS = {F32: 4, BF16: 2}


def _bounds(ap):
    es = _DTS.get(ap.dtype, 4)
    lo = ap.offset
    hi = ap.offset + 1
    for step, num in ap.ap:
        if step >= 0:
            hi += step * (num - 1)
        else:
            lo += step * (num - 1)
    return (ap.tensor.name, lo * es, hi * es)


class Kb:

    def __init__(self, nc):
        self.nc = nc
        self.script = {e: [] for e in ENGS}
        self.sem_handles = {}
        self.cnt = {}
        for e in ("sc", "ve", "te", "po"):
            self.cnt["e_" + e] = 0
        for q in ("sy", "sc", "po"):
            self.cnt["q_" + q] = 0
        for q in ("sy", "sc", "po"):
            self.cnt["f_" + q] = 0
        self.f_cover = {"q_sy": [], "q_sc": [], "q_po": []}
        self.waited = {e: {} for e in ENGS}
        self.acc = {}

    def make_sems(self, stack):
        for name in self.cnt:
            self.sem_handles[name] = stack.enter_context(self.nc.semaphore(name))

    @staticmethod
    def _overlap(a, b):
        return a[0] < b[1] and b[0] < a[1]

    def _need(self, tname, lo, hi, for_write):
        ent = self.acc.get(tname)
        need = {}
        if ent is None:
            return need
        for (l, h), (s, v) in ent["w"].items():
            if self._overlap((lo, hi), (l, h)) and need.get(s, -1) < v:
                need[s] = v
        if for_write:
            for (l, h), readers in ent["r"].items():
                if self._overlap((lo, hi), (l, h)):
                    for s, v in readers.items():
                        if need.get(s, -1) < v:
                            need[s] = v
        return need

    def _rec_write(self, tname, lo, hi, sem, val):
        ent = self.acc.setdefault(tname, {"w": {}, "r": {}})
        for d in (ent["w"], ent["r"]):
            for kk in [kk for kk in list(d) if lo <= kk[0] and kk[1] <= hi]:
                del d[kk]
        ent["w"][(lo, hi)] = (sem, val)

    def _rec_read(self, tname, lo, hi, sem, val):
        ent = self.acc.setdefault(tname, {"w": {}, "r": {}})
        r = ent["r"].setdefault((lo, hi), {})
        if r.get(sem, -1) < val:
            r[sem] = val

    def _emit_fence(self, q):
        eng = q[2:]
        n = self.cnt[q]
        fq = "f_" + eng
        self.cnt[fq] += 1
        sems = self.sem_handles

        def closure(E, q=q, n=n, fq=fq):
            E.wait_ge(sems[q], n)
            E.sem_inc(sems[fq], 1)

        self.script[eng].append(closure)
        self.f_cover[q].append(n)
        w = self.waited[eng]
        if w.get(q, -1) < n:
            w[q] = n
        w[fq] = self.cnt[fq]

    def _waits_for(self, eng, rb, wb, pe_accum=False):
        import bisect
        need = {}
        for t, lo, hi in rb:
            for s, v in self._need(t, lo, hi, False).items():
                if need.get(s, -1) < v:
                    need[s] = v
        for t, lo, hi in wb:
            for s, v in self._need(t, lo, hi, True).items():
                if need.get(s, -1) < v:
                    need[s] = v
        if pe_accum and eng == "te":
            need.pop("e_te", None)
        for s in [s for s in need if s.startswith("q_")]:
            v = need.pop(s)
            cov = self.f_cover[s]
            i = bisect.bisect_left(cov, v)
            if i == len(cov):
                self._emit_fence(s)
                i = len(cov) - 1
            fs = "f_" + s[2:]
            if need.get(fs, -1) < i + 1:
                need[fs] = i + 1
        w = self.waited[eng]
        waits = []
        for s, v in sorted(need.items()):
            if w.get(s, -1) < v:
                waits.append((s, v))
                w[s] = v
        return waits

    def op(self, eng, fn, reads=(), writes=(), pe_accum=False):
        assert eng != "sy", "sync engine only issues DMA"
        rb = [_bounds(a) for a in reads]
        wb = [_bounds(a) for a in writes]
        waits = self._waits_for(eng, rb, wb, pe_accum)
        my = "e_" + eng
        self.cnt[my] += 1
        val = self.cnt[my]
        sems = self.sem_handles

        def closure(E, waits=waits, fn=fn, my=my):
            for s, v in waits:
                E.wait_ge(sems[s], v)
            fn(E).then_inc(sems[my], 1)

        self.script[eng].append(closure)
        for t, lo, hi in rb:
            self._rec_read(t, lo, hi, my, val)
        for t, lo, hi in wb:
            self._rec_write(t, lo, hi, my, val)

    def dma(self, eng, out, in_):
        q = "q_" + eng
        rb = [_bounds(in_)]
        wb = [_bounds(out)]
        waits = self._waits_for(eng, rb, wb)
        self.cnt[q] += 16
        val = self.cnt[q]
        sems = self.sem_handles

        def closure(E, waits=waits, out=out, in_=in_, q=q):
            for s, v in waits:
                E.wait_ge(sems[s], v)
            E.dma_start(out=out, in_=in_).then_inc(sems[q], 16)

        self.script[eng].append(closure)
        t, lo, hi = rb[0]
        self._rec_read(t, lo, hi, q, val)
        t, lo, hi = wb[0]
        self._rec_write(t, lo, hi, q, val)

    def barrier(self):
        sems = self.sem_handles
        snap = dict(self.cnt)
        for e in ENGS:
            waits = []
            for s, v in sorted(snap.items()):
                if v > 0 and self.waited[e].get(s, -1) < v:
                    waits.append((s, v))
                    self.waited[e][s] = v

            def closure(E, waits=waits):
                for s, v in waits:
                    E.wait_ge(sems[s], v)

            self.script[e].append(closure)
        self.acc = {}

    def replay(self, block):
        k = self

        @block.sync
        def _(E):
            for fn in k.script["sy"]:
                fn(E)

        @block.scalar
        def _(E):
            for fn in k.script["sc"]:
                fn(E)

        @block.vector
        def _(E):
            for fn in k.script["ve"]:
                fn(E)

        @block.tensor
        def _(E):
            for fn in k.script["te"]:
                fn(E)

        @block.gpsimd
        def _(E):
            for fn in k.script["po"]:
                fn(E)


def build_program(NT=2048, debug_taps=(), phases=None, feeds=()):
    nc = bass.Bass("TRN2", target_bir_lowering=False)
    TC = min(512, NT)
    NTC = NT // TC
    import contextlib

    dram = {}

    def din(name, shape, dtype=F32):
        dram[name] = nc.dram_tensor(name, list(shape), dtype, kind="ExternalInput").ap()
        return dram[name]

    def dint(name, shape, dtype=F32):
        kind = "ExternalOutput" if name in debug_taps else "Internal"
        if name in feeds:
            kind = "ExternalInput"
        dram[name] = nc.dram_tensor(name, list(shape), dtype, kind=kind).ap()
        return dram[name]

    x_tm = din("x_tm", [NT, D])
    wa_in = din("wa_in", [73, P, 16 * P])
    wa_dt = din("wa_dt", [P, DI])
    wa_out = din("wa_out", [16, P, 32 * P])
    wa_cvd = din("wa_cvd", [32, P, 4 * P])
    a_bd = din("a_bd", [P, 32])
    a_cb = din("a_cb", [P, 32])
    a_al = din("a_al", [P, 32 * NS])
    a_Dp = din("a_Dp", [P, 32])
    wb_in = din("wb_in", [64, P, 16 * P])
    wb_x = din("wb_x", [32, P, 160])
    wb_dt = din("wb_dt", [P, DI])
    wb_out = din("wb_out", [16, P, 32 * P])
    wb_cvd = din("wb_cvd", [32, P, 4 * P])
    b_bd = din("b_bd", [P, 32])
    b_cb = din("b_cb", [P, 32])
    b_al = din("b_al", [P, 32 * NS])
    b_Dp = din("b_Dp", [P, 32])
    wf_gu = din("wf_gu", [128, P, 16 * P])
    wf_pj = din("wf_pj", [16, P, 64 * P])
    selB_d = din("selB", [P, P * P], BF16)
    selC_d = din("selC", [P, NS * P], BF16)
    gsel_d = din("gsel", [P, 2], BF16)
    gselT_d = din("gselT", [2, P], BF16)
    gselTf_d = din("gselTf", [2, P])
    ones_d = din("ones_c", [P, 1])
    onesb_d = din("ones_cb", [P, 1], BF16)
    onesr_d = din("ones_r", [1, P])
    identf_d = din("identf", [P, P])

    out_tm = nc.dram_tensor("out_tm", [NT, D], F32, kind="ExternalOutput").ap()
    dram["out_tm"] = out_tm

    XFa = dint("XFa", [D, NT])
    XFb = dint("XFb", [D, NT])
    XLN = dint("XLN", [D, NT], BF16)
    GZ = dint("GZ", [DI, NT])
    BIGC = dint("BIGC", [DI, NT])
    BMd = dint("BMd", [512, NT])
    BHID = dint("BHID", [DI, NT], BF16)
    X5d = dint("X5d", [512, NT], BF16)
    YFd = dint("YFd", [DI, NT], BF16)
    HSd = dint("HSd", [HF, NT], BF16)
    WGUq = dint("WGUq", [128, P, 16 * P], BF16)
    WPJq = dint("WPJq", [16, P, 64 * P], BF16)
    WCVA = dint("WCVA", [32, P, 4 * P], BF16)
    WCVB = dint("WCVB", [32, P, 4 * P], BF16)

    k = Kb(nc)

    SB = {}

    def sba(name, shape, dtype, off):
        SB[name] = nc.alloc_sbuf_tensor_at(name, list(shape), dtype, offset=off)
        return SB[name]

    A0, B0, C0, D0, E0 = 18560, 84096, 116864, 149632, 157824
    F0, G0, H0 = 181440, 193728, 211136
    xln = sba("xln", [P, 16 * NT], BF16, A0)
    selb = sba("selb", [P, P * P], BF16, A0)
    selc = sba("selc", [P, NS * P], BF16, A0 + 32768)
    bm4 = sba("bm4", [P, 4 * TC], BF16, A0 + 36864)
    usta = sba("usta", [P, 16 * TC], F32, A0)
    ustb = sba("ustb", [P, 16 * TC], F32, A0 + 32768)
    yft = sba("yft", [P, 32 * TC], BF16, A0)
    hnq = sba("hnq", [P, 64 * TC], BF16, A0)
    stg = sba("stg", [P, 16 * TC], F32, B0)
    bcb = sba("bcb", [P, 32 * TC], BF16, B0)
    wq0 = sba("wq0", [P, 16 * P], F32, C0)
    wq1 = sba("wq1", [P, 16 * P], F32, C0 + 8192)
    wb0 = sba("wb0", [P, 32 * P], BF16, C0 + 16384)
    wb1 = sba("wb1", [P, 32 * P], BF16, C0 + 24576)
    wxb = sba("wxb", [P, 32 * 160], BF16, C0)
    selbb = sba("selbb", [32, 32 * P], BF16, C0 + 10240)
    qt0 = sba("qt0", [P, TC], F32, D0)
    qt1 = sba("qt1", [P, TC], F32, D0 + 2048)
    tmin = sba("tmin", [P, D], F32, D0)
    dts = sba("dts", [P, TC], F32, E0)
    dub = sba("dub", [P, TC], BF16, E0 + 2048)
    dab = sba("dab", [P, 2 * TC], BF16, E0 + 3072)
    dbb = sba("dbb", [P, 2 * TC], BF16, E0 + 5120)
    hbb = sba("hbb", [P, 2 * TC], BF16, E0 + 7168)
    ytb = sba("ytb", [P, 2 * TC], BF16, E0 + 9216)
    yac = sba("yac", [P, TC], F32, E0 + 11264)
    yfs = sba("yfs", [P, 2 * TC], BF16, E0 + 13312)
    cin = sba("cin", [P, TC + 8], BF16, E0 + 15360)
    cac = sba("cac", [P, TC], F32, E0 + 17472)
    cmt = sba("cmt", [P, 2 * TC], BF16, E0 + 19520)
    szt = sba("szt", [P, TC], F32, E0 + 21568)
    sgt = sba("sgt", [P, 2 * TC], F32, E0)
    hbt = sba("hbt", [P, 2 * TC], BF16, E0 + 4096)
    hqt = sba("hqt", [P, 2 * TC], BF16, E0 + 6144)
    qb0 = sba("qb0", [P, TC], BF16, E0 + 8192)
    qab = sba("qab", [P, TC], BF16, E0 + 9216)
    qwh = sba("qwh", [P, TC], BF16, E0 + 10240)
    wq2 = sba("wq2", [P, 16 * P], F32, E0)
    qb1 = sba("qb1", [P, TC], BF16, E0 + 11264)
    qa1 = sba("qa1", [P, TC], BF16, E0 + 12288)
    qw1 = sba("qw1", [P, TC], BF16, E0 + 13312)
    qt0b = sba("qt0b", [P, TC], F32, E0 + 14336)
    qt1b = sba("qt1b", [P, TC], F32, E0 + 16384)
    rowA2 = sba("rowA2", [2, TC], F32, E0 + 18432)
    magict = sba("magict", [P, TC], F32, E0 + 20480)
    const2r = sba("const2r", [2, TC], F32, H0 + 14336)
    tp0 = sba("tp0", [P, TC], F32, F0)
    tp1 = sba("tp1", [P, TC], F32, F0 + 2048)
    tp2 = sba("tp2", [P, TC], F32, F0 + 4096)
    dgr = sba("dgr", [P, 2 * 4 * P], BF16, F0 + 6144)
    tp4 = sba("tp4", [P, TC], F32, F0 + 8192)
    tp5 = sba("tp5", [P, TC], F32, F0 + 10240)
    dtl = sba("dtl", [P, NT], BF16, G0)
    wdt = sba("wdt", [P, DI], BF16, G0 + 4096)
    acol = sba("acol", [P, 32 * NS], F32, G0 + 12288)
    carr = sba("carr", [P, 32 * NS], F32, G0 + 14336)
    bcol = sba("bcol", [P, 8 * 32], F32, G0 + 16384)
    rowA = sba("rowA", [2, TC], F32, H0)
    rowB = sba("rowB", [1, TC], F32, H0 + 2048)
    rowC = sba("rowC", [1, TC], F32, H0 + 4096)
    rowD = sba("rowD", [1, TC], F32, H0 + 6144)
    rowE = sba("rowE", [1, TC], F32, H0 + 8192)
    rwb = sba("rwb", [2, TC], BF16, H0 + 10240)
    rw2 = sba("rw2", [2, TC], BF16, H0 + 11264)
    gsel_s = sba("gsel_s", [P, 2], BF16, H0 + 12288)
    onec = sba("onec", [P, 1], F32, H0 + 12320)
    onecb = sba("onecb", [P, 1], BF16, H0 + 12352)
    gselT_s = sba("gselT_s", [2, P], BF16, H0 + 12384)
    gselTf_s = sba("gselTf", [2, P], F32, H0 + 13664)
    oner = sba("oner", [1, P], F32, H0 + 12640)
    idf = sba("idf", [P, P], F32, H0 + 13152)

    def bcol_s(i):
        return bcol[:, i * 32:(i + 1) * 32]

    pb = [nc.alloc_psum_tensor(f"pb{i}", [P, TC], F32) for i in range(6)]
    pc = [nc.alloc_psum_tensor(f"pc{i}", [P, TC], F32) for i in range(2)]

    es = contextlib.ExitStack()
    block = es.enter_context(nc.Block())
    k.make_sems(es)

    def cast(eng, dst, src):
        if eng == "sc":
            k.op("sc", lambda E, d=dst, s=src: E.copy(out=d, in_=s),
                 reads=(src,), writes=(dst,))
        else:
            k.op(eng, lambda E, d=dst, s=src: E.tensor_copy(out=d, in_=s),
                 reads=(src,), writes=(dst,))

    def mm(out, lhsT, rhs, start, stop):
        k.op("te", lambda E, a=lhsT, b=rhs, o=out, s1=start, s2=stop:
             E.matmul(o, a, b, start=s1, stop=s2),
             reads=(lhsT, rhs), writes=(out,), pe_accum=True)

    def act(out, in_, func, bias=0.0, scale=1.0):
        rd = [in_] + [x for x in (bias, scale) if not isinstance(x, (int, float))]
        k.op("sc", lambda E, o=out, i=in_, f=func, b=bias, s=scale:
             E.activation(out=o, in_=i, func=f, bias=b, scale=s),
             reads=tuple(rd), writes=(out,))

    def tt(eng, out, in0, in1, op):
        k.op(eng, lambda E, o=out, a=in0, b=in1, p=op: E.tensor_tensor(o, a, b, p),
             reads=(in0, in1), writes=(out,))

    def ts(eng, out, in0, s1, s2, op0, op1=None):
        rd = [in0] + [x for x in (s1, s2)
                      if x is not None and not isinstance(x, (int, float))]

        def f(E, o=out, i=in0, a=s1, b=s2, p0=op0, p1=op1):
            if p1 is None:
                return E.tensor_scalar(out=o, in0=i, scalar1=a, scalar2=None, op0=p0)
            return E.tensor_scalar(out=o, in0=i, scalar1=a, scalar2=b, op0=p0, op1=p1)

        k.op(eng, f, reads=tuple(rd), writes=(out,))

    def stt(eng, out, in0, scalar, in1, op0, op1):
        rd = [in0, in1] + ([scalar] if not isinstance(scalar, (int, float)) else [])
        k.op(eng, lambda E, o=out, a=in0, s=scalar, b=in1, p0=op0, p1=op1:
             E.scalar_tensor_tensor(out=o, in0=a, scalar=s, in1=b, op0=p0, op1=p1),
             reads=tuple(rd), writes=(out,))

    def scan_op(out, d0, d1, initial):
        rd = [d0, d1] + ([initial] if not isinstance(initial, (int, float)) else [])
        k.op("ve", lambda E, o=out, a=d0, b=d1, i=initial:
             E.tensor_tensor_scan(out=o, data0=a, data1=b, initial=i,
                                  op0=ALU.mult, op1=ALU.add),
             reads=tuple(rd), writes=(out,))

    def recip(out, in_):
        k.op("ve", lambda E, o=out, i=in_: E.reciprocal(out=o, in_=i),
             reads=(in_,), writes=(out,))

    def transpose_pe(out_ps, in_sb):
        k.op("te", lambda E, o=out_ps, i=in_sb: E.transpose(o, i, idf[:, :]),
             reads=(in_sb, idf[:, :]), writes=(out_ps,))

    def memset(eng, ap_, c):
        k.op(eng, lambda E, a=ap_, cc=c: E.memset(a, cc), reads=(), writes=(ap_,))

    def softplus(out, psrc, bias_col):
        act(tp5[:, :TC], psrc, AF.Exp, bias=bias_col)
        act(out, tp5[:, :TC], AF.Ln, bias=1.0)

    def load_wblock(dblk, K, wq, wbf, do_cast=True):
        nk = K // P
        for piece in range(0, nk, 16):
            pe_ = min(piece + 16, nk)
            w = pe_ - piece
            k.dma("sy", wq[:, : w * P], dblk[:, piece * P: pe_ * P])
            if do_cast:
                cast("sc", wbf[:, piece * P: pe_ * P], wq[:, : w * P])

    def ln_tail(tc):
        t0 = tc * TC
        for mc in range(16):
            sl = stg[:, mc * TC:(mc + 1) * TC]
            act(tp0[:, :TC], sl, AF.Square)
            mm(pb[5][0:1, :TC], onec[:, :], sl, start=(mc == 0), stop=(mc == 15))
            mm(pb[4][0:1, :TC], onec[:, :], tp0[:, :TC],
               start=(mc == 0), stop=(mc == 15))
        rm = rowB[0:1, 0:TC]
        rv = rowC[0:1, 0:TC]
        rr = rowD[0:1, 0:TC]
        rb = rowE[0:1, 0:TC]
        ts("ve", rm, pb[5][0:1, :TC], 1.0 / D, None, ALU.mult)
        ts("ve", rv, pb[4][0:1, :TC], 1.0 / D, None, ALU.mult)
        tt("ve", rb, rm, rm, ALU.mult)
        tt("ve", rv, rv, rb, ALU.subtract)
        ts("ve", rv, rv, LN_EPS, None, ALU.add)
        act(rr, rv, AF.Ln)
        act(rr, rr, AF.Exp, scale=-0.5)
        stt("ve", rb, rm, -1.0, rr, ALU.mult, ALU.mult)
        mm(pb[2][:, :TC], oner[:, :], rr, start=True, stop=True)
        mm(pb[3][:, :TC], oner[:, :], rb, start=True, stop=True)
        for mc in range(16):
            sl = stg[:, mc * TC:(mc + 1) * TC]
            tt("ve", tp0[:, :TC], sl, pb[2][:, :TC], ALU.mult)
            tt("ve", cmt[:, :TC], tp0[:, :TC], pb[3][:, :TC], ALU.add)
            k.dma("sc", XLN[mc * P:(mc + 1) * P, t0:t0 + TC], cmt[:, :TC])

    def p0():
        k.dma("sy", gsel_s[:, :], gsel_d)
        k.dma("sy", gselT_s[:, :], gselT_d)
        k.dma("sy", gselTf_s[:, :], gselTf_d)
        k.dma("sy", onec[:, :], ones_d)
        k.dma("sy", onecb[:, :], onesb_d)
        k.dma("sy", oner[:, :], onesr_d)
        k.dma("sy", idf[:, :], identf_d)
        for name, dst in ((a_bd, bcol_s(0)), (a_cb, bcol_s(1)), (a_Dp, bcol_s(2)),
                          (b_bd, bcol_s(3)), (b_cb, bcol_s(4)), (b_Dp, bcol_s(5))):
            k.dma("sy", dst, name)
        ts("ve", bcol_s(0), bcol_s(0), 2.0, None, ALU.mult)
        for src, dst in ((wa_cvd, WCVA), (wb_cvd, WCVB)):
            for ec in range(32):
                wqb, wbb = (wq0, wb0) if ec % 2 == 0 else (wq1, wb1)
                k.dma("sy", wqb[:, :4 * P], src[ec])
                cast("sc", wbb[:, :4 * P], wqb[:, :4 * P])
                k.dma("sc", dst[ec], wbb[:, :4 * P])
        for tc in range(NTC):
            t0 = tc * TC
            for tt_ in range(TC // P):
                tok0 = t0 + tt_ * P
                k.dma("sy", tmin[:, :], x_tm[tok0:tok0 + P, :])
                for fc in range(16):
                    transpose_pe(pb[2][:, 0:P], tmin[:, fc * P:(fc + 1) * P])
                    cast("ve", stg[:, fc * TC + tt_ * P: fc * TC + tt_ * P + P],
                         pb[2][:, 0:P])
            for mc in range(16):
                k.dma("sc", XFa[mc * P:(mc + 1) * P, t0:t0 + TC],
                      stg[:, mc * TC:(mc + 1) * TC])
            ln_tail(tc)
        k.barrier()

    def load_acol(al_d):
        k.dma("sy", acol[:, :], al_d)
        act(acol[:, :], acol[:, :], AF.Exp)
        ts("ve", acol[:, :], acol[:, :], -1.0, None, ALU.mult)

    def load_xln():
        for mc in range(16):
            k.dma("sy", xln[:, mc * NT:(mc + 1) * NT],
                  XLN[mc * P:(mc + 1) * P, :])

    def load_wdt(wdt_d):
        for piece in range(2):
            sl = slice(piece * (DI // 2), (piece + 1) * (DI // 2))
            k.dma("sy", wq0[:, :DI // 2], wdt_d[:, sl])
            cast("sc", wdt[:, sl], wq0[:, :DI // 2])

    def inproj_a():
        load_xln()
        load_wdt(wa_dt)
        for mc in range(73):
            wqb, wbb = (wq0, wb0) if mc % 2 == 0 else (wq1, wb1)
            load_wblock(wa_in[mc], D, wqb, wbb)
            for tc in range(NTC):
                t0 = tc * TC
                po_ = pb[(mc * NTC + tc) % 2]
                for kc in range(16):
                    mm(po_[:, :TC], wbb[:, kc * P:(kc + 1) * P],
                       xln[:, kc * NT + t0: kc * NT + t0 + TC],
                       start=(kc == 0), stop=(kc == 15))
                if mc < 32:
                    act(tp0[:, :TC], po_[:, :TC], AF.Silu)
                    k.dma("sc", GZ[mc * P:(mc + 1) * P, t0:t0 + TC], tp0[:, :TC])
                elif mc < 36:
                    cast("ve", cmt[:, :TC], po_[:, :TC])
                    k.dma("sc", X5d[(mc - 32) * P:(mc - 31) * P, t0:t0 + TC],
                          cmt[:, :TC])
                elif mc < 40:
                    cast("ve", tp1[:, :TC], po_[:, :TC])
                    k.dma("sc", BMd[(mc - 36) * P:(mc - 35) * P, t0:t0 + TC],
                          tp1[:, :TC])
                elif mc < 72:
                    cast("ve", tp1[:, :TC], po_[:, :TC])
                    k.dma("sc", BIGC[(mc - 40) * P:(mc - 39) * P, t0:t0 + TC],
                          tp1[:, :TC])
                else:
                    cast("ve", dtl[:, t0:t0 + TC], po_[:, :TC])
        k.barrier()

    conv_mode = ["a"]

    def conv_u(tc, ec, cv_d, cb_col, out_u):
        t0 = tc * TC
        dg = dgr[:, (ec % 2) * 4 * P:(ec % 2 + 1) * 4 * P]
        k.dma("sy", dg, cv_d[ec])
        if tc == 0:
            memset("ve", cin[:, 0:3], 0.0)
        if conv_mode[0] == "a":
            for rep in range(8):
                srow = 16 * ec
                if tc == 0:
                    k.dma("sy", cin[rep * 16:(rep + 1) * 16, 3:3 + TC],
                          X5d[srow:srow + 16, t0:t0 + TC])
                else:
                    k.dma("sy", cin[rep * 16:(rep + 1) * 16, 0:3 + TC],
                          X5d[srow:srow + 16, t0 - 3:t0 + TC])
        else:
            if tc == 0:
                k.dma("sy", cin[:, 3:3 + TC], BHID[ec * P:(ec + 1) * P, t0:t0 + TC])
            else:
                k.dma("sy", cin[:, 0:3 + TC],
                      BHID[ec * P:(ec + 1) * P, t0 - 3:t0 + TC])
        for j in range(4):
            mm(pb[0][:, :TC], dg[:, j * P:(j + 1) * P], cin[:, j:j + TC],
               start=(j == 0), stop=(j == 3))
        act(out_u, pb[0][:, :TC], AF.Silu, bias=cb_col)

    def scan_a():
        conv_mode[0] = "a"
        load_acol(a_al)
        k.dma("sy", selb[:, :], selB_d)
        k.dma("sy", selc[:, :], selC_d)
        for tc in range(NTC):
            t0 = tc * TC
            for j in range(4):
                k.dma("sy", tp1[:, :TC], BMd[j * P:(j + 1) * P, t0:t0 + TC])
                cast("sc", bm4[:, j * TC:(j + 1) * TC], tp1[:, :TC])
            for ec in range(32):
                mm(pb[1][:, :TC], wdt[:, ec * P:(ec + 1) * P], dtl[:, t0:t0 + TC],
                   start=True, stop=True)
                softplus(dts[:, :TC], pb[1][:, :TC], bcol_s(0)[:, ec:ec + 1])
                conv_u(tc, ec, WCVA, bcol_s(1)[:, ec:ec + 1], cac[:, :TC])
                tt("po", tp2[:, :TC], dts[:, :TC], cac[:, :TC], ALU.mult)
                cast("sc", dub[:, :TC], tp2[:, :TC])
                k.dma("sy", tp1[:, :TC], BIGC[ec * P:(ec + 1) * P, t0:t0 + TC])
                cast("sc", cmt[:, TC:2 * TC], tp1[:, :TC])
                k.dma("sy", szt[:, :TC], GZ[ec * P:(ec + 1) * P, t0:t0 + TC])
                for n in range(NS):
                    nb = n % 2
                    idx = ec * NS + n
                    r = (ec % 8) * NS + n
                    mm(pc[0][:, :TC], selb[:, r * P:(r + 1) * P],
                       bm4[:, (ec // 8) * TC:(ec // 8 + 1) * TC],
                       start=True, stop=True)
                    mm(pc[1][:, :TC], selc[:, n * P:(n + 1) * P],
                       cmt[:, TC:2 * TC], start=True, stop=True)
                    act(dab[:, nb * TC:(nb + 1) * TC], dts[:, :TC], AF.Exp,
                        scale=acol[:, idx:idx + 1])
                    cast("sc", bcb[:, nb * TC:(nb + 1) * TC], pc[0][:, :TC])
                    tt("po", dbb[:, nb * TC:(nb + 1) * TC], dub[:, :TC],
                       bcb[:, nb * TC:(nb + 1) * TC], ALU.mult)
                    scan_op(hbb[:, nb * TC:(nb + 1) * TC],
                            dab[:, nb * TC:(nb + 1) * TC],
                            dbb[:, nb * TC:(nb + 1) * TC],
                            0.0 if tc == 0 else carr[:, idx:idx + 1])
                    if tc < NTC - 1:
                        cast("sc", carr[:, idx:idx + 1],
                             hbb[:, nb * TC + TC - 1: nb * TC + TC])
                    if n == 0:
                        tt("ve", yac[:, :TC], hbb[:, nb * TC:(nb + 1) * TC],
                           pc[1][:, :TC], ALU.mult)
                    else:
                        tt("ve", ytb[:, nb * TC:(nb + 1) * TC],
                           hbb[:, nb * TC:(nb + 1) * TC],
                           pc[1][:, :TC], ALU.mult)
                        tt("po", yac[:, :TC], yac[:, :TC],
                           ytb[:, nb * TC:(nb + 1) * TC], ALU.add)
                stt("ve", tp2[:, :TC], cac[:, :TC], bcol_s(2)[:, ec:ec + 1],
                    yac[:, :TC], ALU.mult, ALU.add)
                tt("ve", yfs[:, (ec % 2) * TC:((ec % 2) + 1) * TC], tp2[:, :TC],
                   szt[:, :TC], ALU.mult)
                k.dma("sc", YFd[ec * P:(ec + 1) * P, t0:t0 + TC],
                      yfs[:, (ec % 2) * TC:((ec % 2) + 1) * TC])
        k.barrier()

    def out_proj(w_blocks, xf_old, xf_new):
        for tc in range(NTC):
            t0 = tc * TC
            for ec in range(32):
                k.dma("sy", yft[:, ec * TC:(ec + 1) * TC],
                      YFd[ec * P:(ec + 1) * P, t0:t0 + TC])
            for mc in range(16):
                wqb, wbb = (wq0, wb0) if mc % 2 == 0 else (wq1, wb1)
                load_wblock(w_blocks[mc], DI, wqb, wbb)
                po_ = pb[mc % 2]
                for kc in range(32):
                    mm(po_[:, :TC], wbb[:, kc * P:(kc + 1) * P],
                       yft[:, kc * TC:(kc + 1) * TC],
                       start=(kc == 0), stop=(kc == 31))
                k.dma("sy", tp1[:, :TC], xf_old[mc * P:(mc + 1) * P, t0:t0 + TC])
                tt("ve", stg[:, mc * TC:(mc + 1) * TC], po_[:, :TC], tp1[:, :TC],
                   ALU.add)
                k.dma("sc", xf_new[mc * P:(mc + 1) * P, t0:t0 + TC],
                      stg[:, mc * TC:(mc + 1) * TC])
            ln_tail(tc)
        k.barrier()

    def quant_group(wqb, wbb, c0, par=0):
        W = TC
        sl = slice(c0, c0 + W)
        if par == 0:
            qb, qa, qw, q0, q1 = qb0, qab, qwh, qt0, qt1
            rA, rB, psum, bc0, bc1 = rowA, rwb, pb[5], pc[0], pc[1]
        else:
            qb, qa, qw, q0, q1 = qb1, qa1, qw1, qt0b, qt1b
            rA, rB, psum, bc0, bc1 = rowA2, rw2, pb[4], pb[2], pb[3]
        cast("sc", qb[:, :W], wqb[:, sl])
        act(qa[:, :W], qb[:, :W], AF.Abs)
        mm(psum[0:2, :W], gsel_s[:, :], qa[:, :W], start=True, stop=True)
        ts("ve", rA[0:2, :W], psum[0:2, :W], 1.0 / 64, None, ALU.mult)
        ts("ve", rB[0:2, :W], rA[0:2, :W], 1e-8, None, ALU.max)
        cast("sc", rA[0:2, :W], rB[0:2, :W])
        act(rA[0:2, :W], rA[0:2, :W], AF.Ln)
        act(rA[0:2, :W], rA[0:2, :W], AF.Exp, scale=-1.0)
        cast("sc", q1[0:2, :W], rB[0:2, :W])
        for _ in range(2):
            tt("po", q0[0:2, :W], q1[0:2, :W], rA[0:2, :W], ALU.mult)
            tt("po", q0[0:2, :W], const2r[0:2, :W], q0[0:2, :W], ALU.subtract)
            tt("po", rA[0:2, :W], rA[0:2, :W], q0[0:2, :W], ALU.mult)
        mm(bc0[:, 0:W], gselTf_s[:, :], rA[0:2, :W], start=True, stop=True)
        mm(bc1[:, 0:W], gselT_s[:, :], rB[0:2, :W], start=True, stop=True)
        tt("ve", qa[:, :W], qb[:, :W], bc0[:, 0:W], ALU.mult)
        tt("po", q1[:, :W], qa[:, :W], magict[:, :W], ALU.add)
        tt("po", q1[:, :W], q1[:, :W], magict[:, :W], ALU.subtract)
        ts("ve", q0[:, :W], q1[:, :W], -1.0, 1.0, ALU.max, ALU.min)
        tt("ve", qw[:, :W], q0[:, :W], bc1[:, 0:W], ALU.mult)
        tt("po", qa[:, :W], qw[:, :W], qb[:, :W], ALU.subtract)
        tt("ve", wbb[:, sl], qb[:, :W], qa[:, :W], ALU.add)

    def quant_all():
        memset("ve", magict[:, :], MAGIC)
        memset("ve", const2r[0:2, :], 2.0)
        wqs = (wq0, wq1, wq2)
        wbs = (wb0, wb1)
        i = 0
        for mc in range(128):
            wqb, wbb = wqs[i % 3], wbs[i % 2]
            k.dma("sy", wqb[:, :16 * P], wf_gu[mc])
            for g, c0 in enumerate(range(0, 16 * P, TC)):
                quant_group(wqb, wbb, c0, par=(2 * i + g) % 2)
            k.dma("sc", WGUq[mc], wbb[:, :16 * P])
            i += 1
        for mc in range(16):
            for piece in range(4):
                wqb, wbb = wqs[i % 3], wbs[i % 2]
                psl = slice(piece * 16 * P, (piece + 1) * 16 * P)
                k.dma("sy", wqb[:, :16 * P], wf_pj[mc][:, psl])
                for g, c0 in enumerate(range(0, 16 * P, TC)):
                    quant_group(wqb, wbb, c0, par=(2 * i + g) % 2)
                k.dma("sc", WPJq[mc][:, psl], wbb[:, :16 * P])
                i += 1
        k.barrier()

    def ffn(final, xf_old, xf_new):
        load_xln()
        for j in range(64):
            for half, mc in ((0, j), (1, 64 + j)):
                wbb = wb0 if half == 0 else wb1
                k.dma("sy", wbb[:, :16 * P], WGUq[mc])
            for tc in range(NTC):
                t0 = tc * TC
                for kc in range(16):
                    mm(pb[0][:, :TC], wb0[:, kc * P:(kc + 1) * P],
                       xln[:, kc * NT + t0:kc * NT + t0 + TC],
                       start=(kc == 0), stop=(kc == 15))
                for kc in range(16):
                    mm(pb[1][:, :TC], wb1[:, kc * P:(kc + 1) * P],
                       xln[:, kc * NT + t0:kc * NT + t0 + TC],
                       start=(kc == 0), stop=(kc == 15))
                bsl = slice((tc % 2) * TC, (tc % 2) * TC + TC)
                act(sgt[:, bsl], pb[0][:, :TC], AF.Silu)
                tt("ve", hbt[:, bsl], sgt[:, bsl], pb[1][:, :TC], ALU.mult)
                k.dma("sc", HSd[j * P:(j + 1) * P, t0:t0 + TC], hbt[:, bsl])
        k.barrier()
        for q in range(NTC):
            t0 = q * TC
            for kc in range(64):
                k.dma("sy", hnq[:, kc * TC:(kc + 1) * TC],
                      HSd[kc * P:(kc + 1) * P, t0:t0 + TC])
                act(hqt[:, :TC], hnq[:, kc * TC:(kc + 1) * TC], AF.Square)
                mm(pb[4][0:1, :TC], onecb[:, :], hqt[:, :TC],
                   start=(kc == 0), stop=(kc == 63))
            ts("ve", rowB[0:1, :TC], pb[4][0:1, :TC], 1.0 / HF, RMS_EPS,
               ALU.mult, ALU.add)
            act(rowD[0:1, :TC], rowB[0:1, :TC], AF.Ln)
            act(rowD[0:1, :TC], rowD[0:1, :TC], AF.Exp, scale=-0.5)
            mm(pb[2][:, :TC], oner[:, :], rowD[0:1, :TC], start=True, stop=True)
            for kc in range(64):
                tt("ve", hnq[:, kc * TC:(kc + 1) * TC],
                   hnq[:, kc * TC:(kc + 1) * TC], pb[2][:, :TC], ALU.mult)
            for mc in range(16):
                wbb = wb0 if mc % 2 == 0 else wb1
                for piece in range(4):
                    psl = slice(piece * 16 * P, (piece + 1) * 16 * P)
                    k.dma("sy", wbb[:, (piece % 2) * 16 * P:
                                     (piece % 2 + 1) * 16 * P],
                          WPJq[mc][:, psl])
                    for kc in range(16):
                        kk = piece * 16 + kc
                        mm(pb[mc % 2][:, :TC],
                           wbb[:, (piece % 2) * 16 * P + kc * P:
                                  (piece % 2) * 16 * P + (kc + 1) * P],
                           hnq[:, kk * TC:(kk + 1) * TC],
                           start=(kk == 0), stop=(kk == 63))
                k.dma("sy", tp1[:, :TC], xf_old[mc * P:(mc + 1) * P, t0:t0 + TC])
                tt("ve", stg[:, mc * TC:(mc + 1) * TC], pb[mc % 2][:, :TC],
                   tp1[:, :TC], ALU.add)
                if not final:
                    k.dma("sc", xf_new[mc * P:(mc + 1) * P, t0:t0 + TC],
                          stg[:, mc * TC:(mc + 1) * TC])
            if final:
                for mc in range(16):
                    for jb in range(TC // P):
                        transpose_pe(pb[2][:, 0:P],
                                     stg[:, mc * TC + jb * P: mc * TC + (jb + 1) * P])
                        cast("ve", tp0[:, 0:P], pb[2][:, 0:P])
                        k.dma("sc", out_tm[t0 + jb * P: t0 + (jb + 1) * P,
                                           mc * P:(mc + 1) * P], tp0[:, 0:P])
            else:
                ln_tail(q)
        k.barrier()

    def inproj_b():
        load_xln()
        load_wdt(wb_dt)
        for mc in range(64):
            wqb, wbb = (wq0, wb0) if mc % 2 == 0 else (wq1, wb1)
            load_wblock(wb_in[mc], D, wqb, wbb)
            for tc in range(NTC):
                t0 = tc * TC
                po_ = pb[(mc * NTC + tc) % 2]
                for kc in range(16):
                    mm(po_[:, :TC], wbb[:, kc * P:(kc + 1) * P],
                       xln[:, kc * NT + t0:kc * NT + t0 + TC],
                       start=(kc == 0), stop=(kc == 15))
                if mc < 32:
                    cast("ve", cmt[:, :TC], po_[:, :TC])
                    k.dma("sc", BHID[mc * P:(mc + 1) * P, t0:t0 + TC],
                          cmt[:, :TC])
                else:
                    act(tp0[:, :TC], po_[:, :TC], AF.Silu)
                    k.dma("sc", GZ[(mc - 32) * P:(mc - 31) * P, t0:t0 + TC],
                          tp0[:, :TC])
        k.barrier()

    def scan_b():
        conv_mode[0] = "b"
        load_acol(b_al)
        k.dma("sy", selbb[:, :], selB_d[0:32, 0:32 * P])
        for ec in range(32):
            k.dma("sy", qt0[:, :160], wb_x[ec])
            cast("sc", wxb[:, ec * 160:(ec + 1) * 160], qt0[:, :160])
        for tc in range(NTC):
            t0 = tc * TC
            for ec in range(32):
                ust = usta if ec < 16 else ustb
                usl = ust[:, (ec % 16) * TC:((ec % 16) + 1) * TC]
                conv_u(tc, ec, WCVB, bcol_s(4)[:, ec:ec + 1], usl)
                cast("sc", dub[:, :TC], usl)
                mm(pb[2][:, :TC], wxb[:, ec * 160: ec * 160 + P], dub[:, :TC],
                   start=(ec == 0), stop=(ec == 31))
                mm(pb[3][0:32, :TC], wxb[:, ec * 160 + P: ec * 160 + 160],
                   dub[:, :TC], start=(ec == 0), stop=(ec == 31))
            cast("ve", dtl[:, t0:t0 + TC], pb[2][:, :TC])
            cast("ve", cmt[0:32, 0:TC], pb[3][0:32, :TC])
            for n in range(NS):
                mm(pc[0][:, :TC], selbb[:, n * P:(n + 1) * P],
                   cmt[0:32, 0:TC], start=True, stop=True)
                cast("sc", bcb[:, (2 * n) * TC:(2 * n + 1) * TC], pc[0][:, :TC])
                mm(pc[1][:, :TC], selbb[:, (NS + n) * P:(NS + n + 1) * P],
                   cmt[0:32, 0:TC], start=True, stop=True)
                cast("sc", bcb[:, (2 * n + 1) * TC:(2 * n + 2) * TC],
                     pc[1][:, :TC])
            for ec in range(32):
                ust = usta if ec < 16 else ustb
                usl = ust[:, (ec % 16) * TC:((ec % 16) + 1) * TC]
                mm(pb[1][:, :TC], wdt[:, ec * P:(ec + 1) * P], dtl[:, t0:t0 + TC],
                   start=True, stop=True)
                softplus(dts[:, :TC], pb[1][:, :TC], bcol_s(3)[:, ec:ec + 1])
                tt("po", tp2[:, :TC], dts[:, :TC], usl, ALU.mult)
                cast("sc", dub[:, :TC], tp2[:, :TC])
                k.dma("sy", szt[:, :TC], GZ[ec * P:(ec + 1) * P, t0:t0 + TC])
                for n in range(NS):
                    nb = n % 2
                    idx = ec * NS + n
                    act(dab[:, nb * TC:(nb + 1) * TC], dts[:, :TC], AF.Exp,
                        scale=acol[:, idx:idx + 1])
                    tt("po", dbb[:, nb * TC:(nb + 1) * TC], dub[:, :TC],
                       bcb[:, (2 * n) * TC:(2 * n + 1) * TC], ALU.mult)
                    scan_op(hbb[:, nb * TC:(nb + 1) * TC],
                            dab[:, nb * TC:(nb + 1) * TC],
                            dbb[:, nb * TC:(nb + 1) * TC],
                            0.0 if tc == 0 else carr[:, idx:idx + 1])
                    if tc < NTC - 1:
                        cast("sc", carr[:, idx:idx + 1],
                             hbb[:, nb * TC + TC - 1: nb * TC + TC])
                    if n == 0:
                        tt("ve", yac[:, :TC], hbb[:, nb * TC:(nb + 1) * TC],
                           bcb[:, (2 * n + 1) * TC:(2 * n + 2) * TC], ALU.mult)
                    else:
                        tt("ve", ytb[:, nb * TC:(nb + 1) * TC],
                           hbb[:, nb * TC:(nb + 1) * TC],
                           bcb[:, (2 * n + 1) * TC:(2 * n + 2) * TC], ALU.mult)
                        tt("po", yac[:, :TC], yac[:, :TC],
                           ytb[:, nb * TC:(nb + 1) * TC], ALU.add)
                stt("ve", tp2[:, :TC], usl, bcol_s(5)[:, ec:ec + 1],
                    yac[:, :TC], ALU.mult, ALU.add)
                tt("ve", yfs[:, (ec % 2) * TC:((ec % 2) + 1) * TC], tp2[:, :TC],
                   szt[:, :TC], ALU.mult)
                k.dma("sc", YFd[ec * P:(ec + 1) * P, t0:t0 + TC],
                      yfs[:, (ec % 2) * TC:((ec % 2) + 1) * TC])
        k.barrier()

    def prep_b():
        for name, dst in ((b_bd, bcol_s(3)), (b_cb, bcol_s(4)), (b_Dp, bcol_s(5))):
            k.dma("sy", dst, name)
        load_wdt(wb_dt)
        k.barrier()

    all_phases = [
        ("prep_b", prep_b),
        ("p0", p0),
        ("quant_all", quant_all),
        ("inproj_a", inproj_a),
        ("scan_a", scan_a),
        ("outproj_a", lambda: out_proj(wa_out, XFa, XFb)),
        ("ffn1", lambda: ffn(final=False, xf_old=XFb, xf_new=XFa)),
        ("inproj_b", inproj_b),
        ("scan_b", scan_b),
        ("outproj_b", lambda: out_proj(wb_out, XFa, XFb)),
        ("ffn2", lambda: ffn(final=True, xf_old=XFb, xf_new=None)),
    ]
    for name, fn in all_phases:
        if phases is None or name in phases:
            fn()
    k.barrier()

    k.replay(block)
    es.close()
    return nc, dram


def _host_shared(inputs):
    f32 = np.float32
    import ml_dtypes
    bf16 = ml_dtypes.bfloat16
    g = lambda n: np.asarray(inputs[n])

    W = g("L0_attn_a_W_in")
    wa_in = np.ascontiguousarray(
        W.reshape(73, P, 16, P).transpose(0, 3, 2, 1).reshape(73, P, 16 * P))
    wa_dt = np.ascontiguousarray(g("L0_attn_a_W_dt").T)
    Wo = g("L0_attn_a_W_out")
    wa_out = np.ascontiguousarray(
        Wo.reshape(16, P, 32, P).transpose(0, 3, 2, 1).reshape(16, P, 32 * P))
    cwa = g("L0_attn_a_conv_w")[:, 0, :]
    wa_cvd = np.zeros((32, P, 4 * P), np.float32)
    pi = np.arange(P)
    for ec in range(32):
        for j in range(4):
            wa_cvd[ec, pi, j * P + pi] = cwa[ec * P:(ec + 1) * P, j]
    a_bd = np.ascontiguousarray(g("L0_attn_a_b_dt").reshape(32, P).T)
    a_cb = np.ascontiguousarray(g("L0_attn_a_conv_b").reshape(32, P).T)
    a_al = np.ascontiguousarray(
        g("L0_attn_a_A_log").reshape(32, P, NS).transpose(1, 0, 2).reshape(P, -1))
    a_Dp = np.ascontiguousarray(g("L0_attn_a_D_param").reshape(32, P).T)

    Wb = g("L0_attn_b_W_in")
    wb_in = np.ascontiguousarray(
        Wb.reshape(16, P, 64, P).transpose(2, 1, 0, 3).reshape(64, P, 16 * P))
    wb_x = np.ascontiguousarray(g("L0_attn_b_W_x").reshape(32, P, 160))
    wb_dt = np.ascontiguousarray(g("L0_attn_b_W_dt"))
    Wbo = g("L0_attn_b_W_out")
    wb_out = np.ascontiguousarray(
        Wbo.reshape(32, P, 16, P).transpose(2, 1, 0, 3).reshape(16, P, 32 * P))
    cwb = g("L0_attn_b_conv_kernel")[:, 0, :]
    wb_cvd = np.zeros((32, P, 4 * P), np.float32)
    for ec in range(32):
        for j in range(4):
            wb_cvd[ec, pi, j * P + pi] = cwb[ec * P:(ec + 1) * P, j]
    b_bd = np.ascontiguousarray(g("L0_attn_b_b_dt").reshape(32, P).T)
    b_cb = np.ascontiguousarray(g("L0_attn_b_conv_bias").reshape(32, P).T)
    b_al = np.ascontiguousarray(
        g("L0_attn_b_A_log").reshape(32, P, NS).transpose(1, 0, 2).reshape(P, -1))
    b_Dp = np.ascontiguousarray(g("L0_attn_b_D").reshape(32, P).T)

    Wg = g("L0_ffn_w_gate_up")
    wf_gu = np.ascontiguousarray(
        Wg.reshape(128, P, 16, P).transpose(0, 3, 2, 1).reshape(128, P, 16 * P))
    Wp = g("L0_ffn_w_proj")
    wf_pj = np.ascontiguousarray(
        Wp.reshape(16, P, 64, P).transpose(0, 3, 2, 1).reshape(16, P, 64 * P))

    selB = np.repeat(np.eye(P, dtype=f32)[:, :, None], P, axis=2).reshape(P, -1)
    selC = np.zeros((P, NS, P), dtype=f32)
    for n in range(NS):
        for m in range(P):
            selC[(m // NS) * NS + n, n, m] = 1.0
    selC = selC.reshape(P, -1)
    gsel = np.zeros((P, 2), f32)
    gsel[:64, 0] = 1.0
    gsel[64:, 1] = 1.0
    gselT = np.ascontiguousarray(gsel.T)

    return dict(
        wa_in=wa_in, wa_dt=wa_dt, wa_out=wa_out, wa_cvd=wa_cvd,
        a_bd=a_bd, a_cb=a_cb, a_al=a_al, a_Dp=a_Dp,
        wb_in=wb_in, wb_x=wb_x, wb_dt=wb_dt, wb_out=wb_out, wb_cvd=wb_cvd,
        b_bd=b_bd, b_cb=b_cb, b_al=b_al, b_Dp=b_Dp,
        wf_gu=wf_gu, wf_pj=wf_pj,
        selB=selB.astype(bf16), selC=selC.astype(bf16),
        gsel=gsel.astype(bf16), gselT=gselT.astype(bf16),
        gselTf=gselT.astype(f32),
        ones_c=np.ones((P, 1), f32), ones_cb=np.ones((P, 1), f32).astype(bf16),
        ones_r=np.ones((1, P), f32), identf=np.eye(P, dtype=f32),
    )


def kernel(**inputs):
    nc, dram = build_program(NT=2048)
    shared = _host_shared(inputs)
    x = np.asarray(inputs["L0_attn_a_hidden_states"], dtype=np.float32)
    in_maps = []
    for b in range(8):
        m = dict(shared)
        m["x_tm"] = np.ascontiguousarray(x[b])
        in_maps.append(m)
    res = run_bass_kernel_spmd(nc, in_maps, core_ids=list(range(8)))
    out = np.stack([r["out_tm"].reshape(2048, 2048) for r in res.results], axis=0)
    return out.astype(np.float32)


# version 06 (final)
# speedup vs baseline: 1.2050x; 1.2050x over previous
import numpy as np
import concourse.bass as bass
import concourse.mybir as mybir
from concourse.bass_utils import run_bass_kernel_spmd

F32 = mybir.dt.float32
BF16 = mybir.dt.bfloat16
AF = mybir.ActivationFunctionType
ALU = mybir.AluOpType

P = 128
D = 2048
DI = 4096
HF = 8192
NS = 16
MAGIC = 12582912.0
LN_EPS = 1e-5
RMS_EPS = 1.1920929e-07

ENGS = ("sy", "sc", "ve", "te", "po")
_DTS = {F32: 4, BF16: 2}


def _bounds(ap):
    es = _DTS.get(ap.dtype, 4)
    lo = ap.offset
    hi = ap.offset + 1
    for step, num in ap.ap:
        if step >= 0:
            hi += step * (num - 1)
        else:
            lo += step * (num - 1)
    return (ap.tensor.name, lo * es, hi * es)


class Kb:

    def __init__(self, nc):
        self.nc = nc
        self.script = {e: [] for e in ENGS}
        self.sem_handles = {}
        self.cnt = {}
        for e in ("sc", "ve", "te", "po"):
            self.cnt["e_" + e] = 0
        for q in ("sy", "sc", "po"):
            self.cnt["q_" + q] = 0
        for q in ("sy", "sc", "po"):
            self.cnt["f_" + q] = 0
        self.f_cover = {"q_sy": [], "q_sc": [], "q_po": []}
        self.waited = {e: {} for e in ENGS}
        self.acc = {}

    def make_sems(self, stack):
        for name in self.cnt:
            self.sem_handles[name] = stack.enter_context(self.nc.semaphore(name))

    @staticmethod
    def _overlap(a, b):
        return a[0] < b[1] and b[0] < a[1]

    def _need(self, tname, lo, hi, for_write):
        ent = self.acc.get(tname)
        need = {}
        if ent is None:
            return need
        for (l, h), (s, v) in ent["w"].items():
            if self._overlap((lo, hi), (l, h)) and need.get(s, -1) < v:
                need[s] = v
        if for_write:
            for (l, h), readers in ent["r"].items():
                if self._overlap((lo, hi), (l, h)):
                    for s, v in readers.items():
                        if need.get(s, -1) < v:
                            need[s] = v
        return need

    def _rec_write(self, tname, lo, hi, sem, val):
        ent = self.acc.setdefault(tname, {"w": {}, "r": {}})
        for d in (ent["w"], ent["r"]):
            for kk in [kk for kk in list(d) if lo <= kk[0] and kk[1] <= hi]:
                del d[kk]
        ent["w"][(lo, hi)] = (sem, val)

    def _rec_read(self, tname, lo, hi, sem, val):
        ent = self.acc.setdefault(tname, {"w": {}, "r": {}})
        r = ent["r"].setdefault((lo, hi), {})
        if r.get(sem, -1) < val:
            r[sem] = val

    def _emit_fence(self, q):
        eng = q[2:]
        n = self.cnt[q]
        fq = "f_" + eng
        self.cnt[fq] += 1
        sems = self.sem_handles

        def closure(E, q=q, n=n, fq=fq):
            E.wait_ge(sems[q], n)
            E.sem_inc(sems[fq], 1)

        self.script[eng].append(closure)
        self.f_cover[q].append(n)
        w = self.waited[eng]
        if w.get(q, -1) < n:
            w[q] = n
        w[fq] = self.cnt[fq]

    def _waits_for(self, eng, rb, wb, pe_accum=False):
        import bisect
        need = {}
        for t, lo, hi in rb:
            for s, v in self._need(t, lo, hi, False).items():
                if need.get(s, -1) < v:
                    need[s] = v
        for t, lo, hi in wb:
            for s, v in self._need(t, lo, hi, True).items():
                if need.get(s, -1) < v:
                    need[s] = v
        if pe_accum and eng == "te":
            need.pop("e_te", None)
        for s in [s for s in need if s.startswith("q_")]:
            v = need.pop(s)
            cov = self.f_cover[s]
            i = bisect.bisect_left(cov, v)
            if i == len(cov):
                self._emit_fence(s)
                i = len(cov) - 1
            fs = "f_" + s[2:]
            if need.get(fs, -1) < i + 1:
                need[fs] = i + 1
        w = self.waited[eng]
        waits = []
        for s, v in sorted(need.items()):
            if w.get(s, -1) < v:
                waits.append((s, v))
                w[s] = v
        return waits

    def op(self, eng, fn, reads=(), writes=(), pe_accum=False):
        assert eng != "sy", "sync engine only issues DMA"
        rb = [_bounds(a) for a in reads]
        wb = [_bounds(a) for a in writes]
        waits = self._waits_for(eng, rb, wb, pe_accum)
        my = "e_" + eng
        self.cnt[my] += 1
        val = self.cnt[my]
        sems = self.sem_handles

        def closure(E, waits=waits, fn=fn, my=my):
            for s, v in waits:
                E.wait_ge(sems[s], v)
            fn(E).then_inc(sems[my], 1)

        self.script[eng].append(closure)
        for t, lo, hi in rb:
            self._rec_read(t, lo, hi, my, val)
        for t, lo, hi in wb:
            self._rec_write(t, lo, hi, my, val)

    def dma(self, eng, out, in_):
        q = "q_" + eng
        rb = [_bounds(in_)]
        wb = [_bounds(out)]
        waits = self._waits_for(eng, rb, wb)
        self.cnt[q] += 16
        val = self.cnt[q]
        sems = self.sem_handles

        def closure(E, waits=waits, out=out, in_=in_, q=q):
            for s, v in waits:
                E.wait_ge(sems[s], v)
            E.dma_start(out=out, in_=in_).then_inc(sems[q], 16)

        self.script[eng].append(closure)
        t, lo, hi = rb[0]
        self._rec_read(t, lo, hi, q, val)
        t, lo, hi = wb[0]
        self._rec_write(t, lo, hi, q, val)

    def barrier(self):
        sems = self.sem_handles
        snap = dict(self.cnt)
        for e in ENGS:
            waits = []
            for s, v in sorted(snap.items()):
                if v > 0 and self.waited[e].get(s, -1) < v:
                    waits.append((s, v))
                    self.waited[e][s] = v

            def closure(E, waits=waits):
                for s, v in waits:
                    E.wait_ge(sems[s], v)

            self.script[e].append(closure)
        self.acc = {}

    def replay(self, block):
        k = self

        @block.sync
        def _(E):
            for fn in k.script["sy"]:
                fn(E)

        @block.scalar
        def _(E):
            for fn in k.script["sc"]:
                fn(E)

        @block.vector
        def _(E):
            for fn in k.script["ve"]:
                fn(E)

        @block.tensor
        def _(E):
            for fn in k.script["te"]:
                fn(E)

        @block.gpsimd
        def _(E):
            for fn in k.script["po"]:
                fn(E)


def build_program(NT=2048, debug_taps=(), phases=None, feeds=()):
    nc = bass.Bass("TRN2", target_bir_lowering=False)
    TC = min(512, NT)
    NTC = NT // TC
    import contextlib

    dram = {}

    def din(name, shape, dtype=F32):
        dram[name] = nc.dram_tensor(name, list(shape), dtype, kind="ExternalInput").ap()
        return dram[name]

    def dint(name, shape, dtype=F32):
        kind = "ExternalOutput" if name in debug_taps else "Internal"
        if name in feeds:
            kind = "ExternalInput"
        dram[name] = nc.dram_tensor(name, list(shape), dtype, kind=kind).ap()
        return dram[name]

    x_tm = din("x_tm", [NT, D])
    wa_in = din("wa_in", [73, P, 16 * P])
    wa_dt = din("wa_dt", [P, DI])
    wa_out = din("wa_out", [16, P, 32 * P])
    wa_cvd = din("wa_cvd", [32, P, 4 * P])
    a_bd = din("a_bd", [P, 32])
    a_cb = din("a_cb", [P, 32])
    a_al = din("a_al", [P, 32 * NS])
    a_Dp = din("a_Dp", [P, 32])
    wb_in = din("wb_in", [64, P, 16 * P])
    wb_x = din("wb_x", [32, P, 160])
    wb_dt = din("wb_dt", [P, DI])
    wb_out = din("wb_out", [16, P, 32 * P])
    wb_cvd = din("wb_cvd", [32, P, 4 * P])
    b_bd = din("b_bd", [P, 32])
    b_cb = din("b_cb", [P, 32])
    b_al = din("b_al", [P, 32 * NS])
    b_Dp = din("b_Dp", [P, 32])
    wf_gu = din("wf_gu", [128, P, 16 * P])
    wf_pj = din("wf_pj", [16, P, 64 * P])
    selB_d = din("selB", [P, P * P], BF16)
    selC_d = din("selC", [P, NS * P], BF16)
    gsel_d = din("gsel", [P, 2], BF16)
    gselT_d = din("gselT", [2, P], BF16)
    gselTf_d = din("gselTf", [2, P])
    ones_d = din("ones_c", [P, 1])
    onesb_d = din("ones_cb", [P, 1], BF16)
    onesr_d = din("ones_r", [1, P])
    identf_d = din("identf", [P, P])

    out_tm = nc.dram_tensor("out_tm", [NT, D], F32, kind="ExternalOutput").ap()
    dram["out_tm"] = out_tm

    XFa = dint("XFa", [D, NT])
    XFb = dint("XFb", [D, NT])
    XLN = dint("XLN", [D, NT], BF16)
    GZ = dint("GZ", [DI, NT])
    BIGC = dint("BIGC", [DI, NT])
    BMd = dint("BMd", [512, NT])
    BHID = dint("BHID", [DI, NT], BF16)
    X5d = dint("X5d", [512, NT], BF16)
    YFd = dint("YFd", [DI, NT], BF16)
    HSd = dint("HSd", [HF, NT], BF16)
    WGUq = dint("WGUq", [128, P, 16 * P], BF16)
    WPJq = dint("WPJq", [16, P, 64 * P], BF16)
    WCVA = dint("WCVA", [32, P, 4 * P], BF16)
    WCVB = dint("WCVB", [32, P, 4 * P], BF16)

    k = Kb(nc)

    SB = {}

    def sba(name, shape, dtype, off):
        SB[name] = nc.alloc_sbuf_tensor_at(name, list(shape), dtype, offset=off)
        return SB[name]

    A0, B0, C0, D0, E0 = 18560, 84096, 116864, 149632, 157824
    F0, G0, H0 = 181440, 193728, 211136
    xln = sba("xln", [P, 16 * NT], BF16, A0)
    selb = sba("selb", [P, P * P], BF16, A0)
    selc = sba("selc", [P, NS * P], BF16, A0 + 32768)
    bm4 = sba("bm4", [P, 4 * TC], BF16, A0 + 36864)
    usta = sba("usta", [P, 16 * TC], F32, A0)
    ustb = sba("ustb", [P, 16 * TC], F32, A0 + 32768)
    yft = sba("yft", [P, 32 * TC], BF16, A0)
    hnq = sba("hnq", [P, 64 * TC], BF16, A0)
    stg = sba("stg", [P, 16 * TC], F32, B0)
    bcb = sba("bcb", [P, 32 * TC], BF16, B0)
    wq0 = sba("wq0", [P, 16 * P], F32, C0)
    wq1 = sba("wq1", [P, 16 * P], F32, C0 + 8192)
    wb0 = sba("wb0", [P, 32 * P], BF16, C0 + 16384)
    wb1 = sba("wb1", [P, 32 * P], BF16, C0 + 24576)
    wxb = sba("wxb", [P, 32 * 160], BF16, C0)
    selbb = sba("selbb", [32, 32 * P], BF16, C0 + 10240)
    qt0 = sba("qt0", [P, TC], F32, D0)
    qt1 = sba("qt1", [P, TC], F32, D0 + 2048)
    tmin = sba("tmin", [P, D], F32, D0)
    dts = sba("dts", [P, TC], F32, E0)
    dub = sba("dub", [P, TC], BF16, E0 + 2048)
    dab = sba("dab", [P, 2 * TC], BF16, E0 + 3072)
    dbb = sba("dbb", [P, 2 * TC], BF16, E0 + 5120)
    hbb = sba("hbb", [P, 2 * TC], BF16, E0 + 7168)
    ytb = sba("ytb", [P, 2 * TC], BF16, E0 + 9216)
    yac = sba("yac", [P, TC], F32, E0 + 11264)
    yfs = sba("yfs", [P, 2 * TC], BF16, E0 + 13312)
    cin = sba("cin", [P, TC + 8], BF16, E0 + 15360)
    cac = sba("cac", [P, TC], F32, E0 + 17472)
    cmt = sba("cmt", [P, 2 * TC], BF16, E0 + 19520)
    szt = sba("szt", [P, TC], F32, E0 + 21568)
    sgt = sba("sgt", [P, 2 * TC], F32, E0)
    hbt = sba("hbt", [P, 2 * TC], BF16, E0 + 4096)
    hqt = sba("hqt", [P, 2 * TC], BF16, E0 + 6144)
    qb0 = sba("qb0", [P, TC], BF16, E0 + 8192)
    qab = sba("qab", [P, TC], BF16, E0 + 9216)
    qwh = sba("qwh", [P, TC], BF16, E0 + 10240)
    wq2 = sba("wq2", [P, 16 * P], F32, E0)
    qb1 = sba("qb1", [P, TC], BF16, E0 + 11264)
    qa1 = sba("qa1", [P, TC], BF16, E0 + 12288)
    qw1 = sba("qw1", [P, TC], BF16, E0 + 13312)
    qt0b = sba("qt0b", [P, TC], F32, E0 + 14336)
    qt1b = sba("qt1b", [P, TC], F32, E0 + 16384)
    rowA2 = sba("rowA2", [2, TC], F32, E0 + 18432)
    magict = sba("magict", [P, TC], F32, E0 + 20480)
    const2r = sba("const2r", [2, TC], F32, H0 + 14336)
    tp0 = sba("tp0", [P, TC], F32, F0)
    tp1 = sba("tp1", [P, TC], F32, F0 + 2048)
    tp2 = sba("tp2", [P, TC], F32, F0 + 4096)
    dgr = sba("dgr", [P, 2 * 4 * P], BF16, F0 + 6144)
    tp4 = sba("tp4", [P, TC], F32, F0 + 8192)
    tp5 = sba("tp5", [P, TC], F32, F0 + 10240)
    dtl = sba("dtl", [P, NT], BF16, G0)
    wdt = sba("wdt", [P, DI], BF16, G0 + 4096)
    acol = sba("acol", [P, 32 * NS], F32, G0 + 12288)
    carr = sba("carr", [P, 32 * NS], F32, G0 + 14336)
    bcol = sba("bcol", [P, 8 * 32], F32, G0 + 16384)
    rowA = sba("rowA", [2, TC], F32, H0)
    rowB = sba("rowB", [1, TC], F32, H0 + 2048)
    rowC = sba("rowC", [1, TC], F32, H0 + 4096)
    rowD = sba("rowD", [1, TC], F32, H0 + 6144)
    rowE = sba("rowE", [1, TC], F32, H0 + 8192)
    rwb = sba("rwb", [2, TC], BF16, H0 + 10240)
    rw2 = sba("rw2", [2, TC], BF16, H0 + 11264)
    gsel_s = sba("gsel_s", [P, 2], BF16, H0 + 12288)
    onec = sba("onec", [P, 1], F32, H0 + 12320)
    onecb = sba("onecb", [P, 1], BF16, H0 + 12352)
    gselT_s = sba("gselT_s", [2, P], BF16, H0 + 12384)
    gselTf_s = sba("gselTf", [2, P], F32, H0 + 13664)
    oner = sba("oner", [1, P], F32, H0 + 12640)
    idf = sba("idf", [P, P], F32, H0 + 13152)

    def bcol_s(i):
        return bcol[:, i * 32:(i + 1) * 32]

    pb = [nc.alloc_psum_tensor(f"pb{i}", [P, TC], F32) for i in range(6)]
    pc = [nc.alloc_psum_tensor(f"pc{i}", [P, TC], F32) for i in range(2)]

    es = contextlib.ExitStack()
    block = es.enter_context(nc.Block())
    k.make_sems(es)

    def cast(eng, dst, src):
        if eng == "sc":
            k.op("sc", lambda E, d=dst, s=src: E.copy(out=d, in_=s),
                 reads=(src,), writes=(dst,))
        else:
            k.op(eng, lambda E, d=dst, s=src: E.tensor_copy(out=d, in_=s),
                 reads=(src,), writes=(dst,))

    def mm(out, lhsT, rhs, start, stop):
        k.op("te", lambda E, a=lhsT, b=rhs, o=out, s1=start, s2=stop:
             E.matmul(o, a, b, start=s1, stop=s2),
             reads=(lhsT, rhs), writes=(out,), pe_accum=True)

    def act(out, in_, func, bias=0.0, scale=1.0):
        rd = [in_] + [x for x in (bias, scale) if not isinstance(x, (int, float))]
        k.op("sc", lambda E, o=out, i=in_, f=func, b=bias, s=scale:
             E.activation(out=o, in_=i, func=f, bias=b, scale=s),
             reads=tuple(rd), writes=(out,))

    def tt(eng, out, in0, in1, op):
        k.op(eng, lambda E, o=out, a=in0, b=in1, p=op: E.tensor_tensor(o, a, b, p),
             reads=(in0, in1), writes=(out,))

    def ts(eng, out, in0, s1, s2, op0, op1=None):
        rd = [in0] + [x for x in (s1, s2)
                      if x is not None and not isinstance(x, (int, float))]

        def f(E, o=out, i=in0, a=s1, b=s2, p0=op0, p1=op1):
            if p1 is None:
                return E.tensor_scalar(out=o, in0=i, scalar1=a, scalar2=None, op0=p0)
            return E.tensor_scalar(out=o, in0=i, scalar1=a, scalar2=b, op0=p0, op1=p1)

        k.op(eng, f, reads=tuple(rd), writes=(out,))

    def stt(eng, out, in0, scalar, in1, op0, op1):
        rd = [in0, in1] + ([scalar] if not isinstance(scalar, (int, float)) else [])
        k.op(eng, lambda E, o=out, a=in0, s=scalar, b=in1, p0=op0, p1=op1:
             E.scalar_tensor_tensor(out=o, in0=a, scalar=s, in1=b, op0=p0, op1=p1),
             reads=tuple(rd), writes=(out,))

    def scan_op(out, d0, d1, initial):
        rd = [d0, d1] + ([initial] if not isinstance(initial, (int, float)) else [])
        k.op("ve", lambda E, o=out, a=d0, b=d1, i=initial:
             E.tensor_tensor_scan(out=o, data0=a, data1=b, initial=i,
                                  op0=ALU.mult, op1=ALU.add),
             reads=tuple(rd), writes=(out,))

    def recip(out, in_):
        k.op("ve", lambda E, o=out, i=in_: E.reciprocal(out=o, in_=i),
             reads=(in_,), writes=(out,))

    def transpose_pe(out_ps, in_sb):
        k.op("te", lambda E, o=out_ps, i=in_sb: E.transpose(o, i, idf[:, :]),
             reads=(in_sb, idf[:, :]), writes=(out_ps,))

    def memset(eng, ap_, c):
        k.op(eng, lambda E, a=ap_, cc=c: E.memset(a, cc), reads=(), writes=(ap_,))

    def softplus(out, psrc, bias_col):
        act(tp5[:, :TC], psrc, AF.Exp, bias=bias_col)
        act(out, tp5[:, :TC], AF.Ln, bias=1.0)

    def load_wblock(dblk, K, wq, wbf, do_cast=True):
        nk = K // P
        for piece in range(0, nk, 16):
            pe_ = min(piece + 16, nk)
            w = pe_ - piece
            k.dma("sy", wq[:, : w * P], dblk[:, piece * P: pe_ * P])
            if do_cast:
                cast("sc", wbf[:, piece * P: pe_ * P], wq[:, : w * P])

    def ln_tail(tc):
        t0 = tc * TC
        for mc in range(16):
            sl = stg[:, mc * TC:(mc + 1) * TC]
            act(tp0[:, :TC], sl, AF.Square)
            mm(pb[5][0:1, :TC], onec[:, :], sl, start=(mc == 0), stop=(mc == 15))
            mm(pb[4][0:1, :TC], onec[:, :], tp0[:, :TC],
               start=(mc == 0), stop=(mc == 15))
        rm = rowB[0:1, 0:TC]
        rv = rowC[0:1, 0:TC]
        rr = rowD[0:1, 0:TC]
        rb = rowE[0:1, 0:TC]
        ts("ve", rm, pb[5][0:1, :TC], 1.0 / D, None, ALU.mult)
        ts("ve", rv, pb[4][0:1, :TC], 1.0 / D, None, ALU.mult)
        tt("ve", rb, rm, rm, ALU.mult)
        tt("ve", rv, rv, rb, ALU.subtract)
        ts("ve", rv, rv, LN_EPS, None, ALU.add)
        act(rr, rv, AF.Ln)
        act(rr, rr, AF.Exp, scale=-0.5)
        stt("ve", rb, rm, -1.0, rr, ALU.mult, ALU.mult)
        mm(pb[2][:, :TC], oner[:, :], rr, start=True, stop=True)
        mm(pb[3][:, :TC], oner[:, :], rb, start=True, stop=True)
        for mc in range(16):
            sl = stg[:, mc * TC:(mc + 1) * TC]
            tt("ve", tp0[:, :TC], sl, pb[2][:, :TC], ALU.mult)
            tt("ve", cmt[:, :TC], tp0[:, :TC], pb[3][:, :TC], ALU.add)
            k.dma("sc", XLN[mc * P:(mc + 1) * P, t0:t0 + TC], cmt[:, :TC])

    def p0():
        k.dma("sy", gsel_s[:, :], gsel_d)
        k.dma("sy", gselT_s[:, :], gselT_d)
        k.dma("sy", gselTf_s[:, :], gselTf_d)
        k.dma("sy", onec[:, :], ones_d)
        k.dma("sy", onecb[:, :], onesb_d)
        k.dma("sy", oner[:, :], onesr_d)
        k.dma("sy", idf[:, :], identf_d)
        for name, dst in ((a_bd, bcol_s(0)), (a_cb, bcol_s(1)), (a_Dp, bcol_s(2)),
                          (b_bd, bcol_s(3)), (b_cb, bcol_s(4)), (b_Dp, bcol_s(5))):
            k.dma("sy", dst, name)
        ts("ve", bcol_s(0), bcol_s(0), 2.0, None, ALU.mult)
        for src, dst in ((wa_cvd, WCVA), (wb_cvd, WCVB)):
            for ec in range(32):
                wqb, wbb = (wq0, wb0) if ec % 2 == 0 else (wq1, wb1)
                k.dma("sy", wqb[:, :4 * P], src[ec])
                cast("sc", wbb[:, :4 * P], wqb[:, :4 * P])
                k.dma("sc", dst[ec], wbb[:, :4 * P])
        for tc in range(NTC):
            t0 = tc * TC
            for tt_ in range(TC // P):
                tok0 = t0 + tt_ * P
                k.dma("sy", tmin[:, :], x_tm[tok0:tok0 + P, :])
                for fc in range(16):
                    transpose_pe(pb[2][:, 0:P], tmin[:, fc * P:(fc + 1) * P])
                    cast("ve", stg[:, fc * TC + tt_ * P: fc * TC + tt_ * P + P],
                         pb[2][:, 0:P])
            for mc in range(16):
                k.dma("sc", XFa[mc * P:(mc + 1) * P, t0:t0 + TC],
                      stg[:, mc * TC:(mc + 1) * TC])
            ln_tail(tc)
        k.barrier()

    def load_acol(al_d):
        k.dma("sy", acol[:, :], al_d)
        act(acol[:, :], acol[:, :], AF.Exp)
        ts("ve", acol[:, :], acol[:, :], -1.0, None, ALU.mult)

    def load_xln():
        for mc in range(16):
            k.dma("sy", xln[:, mc * NT:(mc + 1) * NT],
                  XLN[mc * P:(mc + 1) * P, :])

    def load_wdt(wdt_d):
        for piece in range(2):
            sl = slice(piece * (DI // 2), (piece + 1) * (DI // 2))
            k.dma("sy", wq0[:, :DI // 2], wdt_d[:, sl])
            cast("sc", wdt[:, sl], wq0[:, :DI // 2])

    def inproj_a():
        load_xln()
        load_wdt(wa_dt)
        for mc in range(73):
            wqb, wbb = (wq0, wb0) if mc % 2 == 0 else (wq1, wb1)
            load_wblock(wa_in[mc], D, wqb, wbb)
            for tc in range(NTC):
                t0 = tc * TC
                po_ = pb[(mc * NTC + tc) % 2]
                for kc in range(16):
                    mm(po_[:, :TC], wbb[:, kc * P:(kc + 1) * P],
                       xln[:, kc * NT + t0: kc * NT + t0 + TC],
                       start=(kc == 0), stop=(kc == 15))
                if mc < 32:
                    act(tp0[:, :TC], po_[:, :TC], AF.Silu)
                    k.dma("sc", GZ[mc * P:(mc + 1) * P, t0:t0 + TC], tp0[:, :TC])
                elif mc < 36:
                    cast("ve", cmt[:, :TC], po_[:, :TC])
                    k.dma("sc", X5d[(mc - 32) * P:(mc - 31) * P, t0:t0 + TC],
                          cmt[:, :TC])
                elif mc < 40:
                    cast("ve", tp1[:, :TC], po_[:, :TC])
                    k.dma("sc", BMd[(mc - 36) * P:(mc - 35) * P, t0:t0 + TC],
                          tp1[:, :TC])
                elif mc < 72:
                    cast("ve", tp1[:, :TC], po_[:, :TC])
                    k.dma("sc", BIGC[(mc - 40) * P:(mc - 39) * P, t0:t0 + TC],
                          tp1[:, :TC])
                else:
                    cast("ve", dtl[:, t0:t0 + TC], po_[:, :TC])
        k.barrier()

    conv_mode = ["a"]

    def conv_u(tc, ec, cv_d, cb_col, out_u):
        t0 = tc * TC
        dg = dgr[:, (ec % 2) * 4 * P:(ec % 2 + 1) * 4 * P]
        k.dma("sy", dg, cv_d[ec])
        if tc == 0:
            memset("ve", cin[:, 0:3], 0.0)
        if conv_mode[0] == "a":
            for rep in range(8):
                srow = 16 * ec
                if tc == 0:
                    k.dma("sy", cin[rep * 16:(rep + 1) * 16, 3:3 + TC],
                          X5d[srow:srow + 16, t0:t0 + TC])
                else:
                    k.dma("sy", cin[rep * 16:(rep + 1) * 16, 0:3 + TC],
                          X5d[srow:srow + 16, t0 - 3:t0 + TC])
        else:
            if tc == 0:
                k.dma("sy", cin[:, 3:3 + TC], BHID[ec * P:(ec + 1) * P, t0:t0 + TC])
            else:
                k.dma("sy", cin[:, 0:3 + TC],
                      BHID[ec * P:(ec + 1) * P, t0 - 3:t0 + TC])
        for j in range(4):
            mm(pb[0][:, :TC], dg[:, j * P:(j + 1) * P], cin[:, j:j + TC],
               start=(j == 0), stop=(j == 3))
        act(out_u, pb[0][:, :TC], AF.Silu, bias=cb_col)

    def scan_a():
        conv_mode[0] = "a"
        load_acol(a_al)
        k.dma("sy", selb[:, :], selB_d)
        k.dma("sy", selc[:, :], selC_d)
        for tc in range(NTC):
            t0 = tc * TC
            for j in range(4):
                k.dma("sy", tp1[:, :TC], BMd[j * P:(j + 1) * P, t0:t0 + TC])
                cast("sc", bm4[:, j * TC:(j + 1) * TC], tp1[:, :TC])
            for ec in range(32):
                mm(pb[1][:, :TC], wdt[:, ec * P:(ec + 1) * P], dtl[:, t0:t0 + TC],
                   start=True, stop=True)
                softplus(dts[:, :TC], pb[1][:, :TC], bcol_s(0)[:, ec:ec + 1])
                conv_u(tc, ec, WCVA, bcol_s(1)[:, ec:ec + 1], cac[:, :TC])
                tt("po", tp2[:, :TC], dts[:, :TC], cac[:, :TC], ALU.mult)
                cast("sc", dub[:, :TC], tp2[:, :TC])
                k.dma("sy", tp1[:, :TC], BIGC[ec * P:(ec + 1) * P, t0:t0 + TC])
                cast("sc", cmt[:, TC:2 * TC], tp1[:, :TC])
                k.dma("sy", szt[:, :TC], GZ[ec * P:(ec + 1) * P, t0:t0 + TC])
                for n in range(NS):
                    nb = n % 2
                    idx = ec * NS + n
                    r = (ec % 8) * NS + n
                    mm(pc[0][:, :TC], selb[:, r * P:(r + 1) * P],
                       bm4[:, (ec // 8) * TC:(ec // 8 + 1) * TC],
                       start=True, stop=True)
                    mm(pc[1][:, :TC], selc[:, n * P:(n + 1) * P],
                       cmt[:, TC:2 * TC], start=True, stop=True)
                    act(dab[:, nb * TC:(nb + 1) * TC], dts[:, :TC], AF.Exp,
                        scale=acol[:, idx:idx + 1])
                    tt("ve", dbb[:, nb * TC:(nb + 1) * TC], dub[:, :TC],
                       pc[0][:, :TC], ALU.mult)
                    scan_op(hbb[:, nb * TC:(nb + 1) * TC],
                            dab[:, nb * TC:(nb + 1) * TC],
                            dbb[:, nb * TC:(nb + 1) * TC],
                            0.0 if tc == 0 else carr[:, idx:idx + 1])
                    if tc < NTC - 1:
                        cast("sc", carr[:, idx:idx + 1],
                             hbb[:, nb * TC + TC - 1: nb * TC + TC])
                    if n == 0:
                        tt("ve", yac[:, :TC], hbb[:, nb * TC:(nb + 1) * TC],
                           pc[1][:, :TC], ALU.mult)
                    else:
                        tt("ve", ytb[:, nb * TC:(nb + 1) * TC],
                           hbb[:, nb * TC:(nb + 1) * TC],
                           pc[1][:, :TC], ALU.mult)
                        tt("po", yac[:, :TC], yac[:, :TC],
                           ytb[:, nb * TC:(nb + 1) * TC], ALU.add)
                stt("ve", tp2[:, :TC], cac[:, :TC], bcol_s(2)[:, ec:ec + 1],
                    yac[:, :TC], ALU.mult, ALU.add)
                tt("ve", yfs[:, (ec % 2) * TC:((ec % 2) + 1) * TC], tp2[:, :TC],
                   szt[:, :TC], ALU.mult)
                k.dma("sc", YFd[ec * P:(ec + 1) * P, t0:t0 + TC],
                      yfs[:, (ec % 2) * TC:((ec % 2) + 1) * TC])
        k.barrier()

    def out_proj(w_blocks, xf_old, xf_new):
        for tc in range(NTC):
            t0 = tc * TC
            for ec in range(32):
                k.dma("sy", yft[:, ec * TC:(ec + 1) * TC],
                      YFd[ec * P:(ec + 1) * P, t0:t0 + TC])
            for mc in range(16):
                wqb, wbb = (wq0, wb0) if mc % 2 == 0 else (wq1, wb1)
                load_wblock(w_blocks[mc], DI, wqb, wbb)
                po_ = pb[mc % 2]
                for kc in range(32):
                    mm(po_[:, :TC], wbb[:, kc * P:(kc + 1) * P],
                       yft[:, kc * TC:(kc + 1) * TC],
                       start=(kc == 0), stop=(kc == 31))
                k.dma("sy", tp1[:, :TC], xf_old[mc * P:(mc + 1) * P, t0:t0 + TC])
                tt("ve", stg[:, mc * TC:(mc + 1) * TC], po_[:, :TC], tp1[:, :TC],
                   ALU.add)
                k.dma("sc", xf_new[mc * P:(mc + 1) * P, t0:t0 + TC],
                      stg[:, mc * TC:(mc + 1) * TC])
            ln_tail(tc)
        k.barrier()

    def quant_group(wqb, wbb, c0, par=0):
        W = TC
        sl = slice(c0, c0 + W)
        if par == 0:
            qb, qa, qw, q0, q1 = qb0, qab, qwh, qt0, qt1
            rA, rB, psum, bc0, bc1 = rowA, rwb, pb[5], pc[0], pc[1]
        else:
            qb, qa, qw, q0, q1 = qb1, qa1, qw1, qt0b, qt1b
            rA, rB, psum, bc0, bc1 = rowA2, rw2, pb[4], pb[2], pb[3]
        cast("sc", qb[:, :W], wqb[:, sl])
        act(qa[:, :W], qb[:, :W], AF.Abs)
        mm(psum[0:2, :W], gsel_s[:, :], qa[:, :W], start=True, stop=True)
        ts("ve", rA[0:2, :W], psum[0:2, :W], 1.0 / 64, None, ALU.mult)
        ts("ve", rB[0:2, :W], rA[0:2, :W], 1e-8, None, ALU.max)
        cast("sc", rA[0:2, :W], rB[0:2, :W])
        act(rA[0:2, :W], rA[0:2, :W], AF.Ln)
        act(rA[0:2, :W], rA[0:2, :W], AF.Exp, scale=-1.0)
        cast("sc", q1[0:2, :W], rB[0:2, :W])
        for _ in range(2):
            tt("ve", q0[0:2, :W], q1[0:2, :W], rA[0:2, :W], ALU.mult)
            ts("ve", q0[0:2, :W], q0[0:2, :W], -1.0, 2.0, ALU.mult, ALU.add)
            tt("ve", rA[0:2, :W], rA[0:2, :W], q0[0:2, :W], ALU.mult)
        mm(bc0[:, 0:W], gselTf_s[:, :], rA[0:2, :W], start=True, stop=True)
        mm(bc1[:, 0:W], gselT_s[:, :], rB[0:2, :W], start=True, stop=True)
        tt("ve", qa[:, :W], qb[:, :W], bc0[:, 0:W], ALU.mult)
        act(q1[:, :W], qa[:, :W], AF.Copy, bias=MAGIC)
        act(q1[:, :W], q1[:, :W], AF.Copy, bias=-MAGIC)
        ts("ve", q0[:, :W], q1[:, :W], -1.0, 1.0, ALU.max, ALU.min)
        tt("ve", qw[:, :W], q0[:, :W], bc1[:, 0:W], ALU.mult)
        tt("po", qa[:, :W], qw[:, :W], qb[:, :W], ALU.subtract)
        tt("ve", wbb[:, sl], qb[:, :W], qa[:, :W], ALU.add)

    def quant_all():
        memset("ve", magict[:, :], MAGIC)
        memset("ve", const2r[0:2, :], 2.0)
        wqs = (wq0, wq1, wq2)
        wbs = (wb0, wb1)
        i = 0
        for mc in range(128):
            wqb, wbb = wqs[i % 3], wbs[i % 2]
            k.dma("sy", wqb[:, :16 * P], wf_gu[mc])
            for g, c0 in enumerate(range(0, 16 * P, TC)):
                quant_group(wqb, wbb, c0, par=(2 * i + g) % 2)
            k.dma("sc", WGUq[mc], wbb[:, :16 * P])
            i += 1
        for mc in range(16):
            for piece in range(4):
                wqb, wbb = wqs[i % 3], wbs[i % 2]
                psl = slice(piece * 16 * P, (piece + 1) * 16 * P)
                k.dma("sy", wqb[:, :16 * P], wf_pj[mc][:, psl])
                for g, c0 in enumerate(range(0, 16 * P, TC)):
                    quant_group(wqb, wbb, c0, par=(2 * i + g) % 2)
                k.dma("sc", WPJq[mc][:, psl], wbb[:, :16 * P])
                i += 1
        k.barrier()

    def ffn(final, xf_old, xf_new):
        load_xln()
        for j in range(64):
            for half, mc in ((0, j), (1, 64 + j)):
                wbb = wb0 if half == 0 else wb1
                k.dma("sy", wbb[:, :16 * P], WGUq[mc])
            for tc in range(NTC):
                t0 = tc * TC
                for kc in range(16):
                    mm(pb[0][:, :TC], wb0[:, kc * P:(kc + 1) * P],
                       xln[:, kc * NT + t0:kc * NT + t0 + TC],
                       start=(kc == 0), stop=(kc == 15))
                for kc in range(16):
                    mm(pb[1][:, :TC], wb1[:, kc * P:(kc + 1) * P],
                       xln[:, kc * NT + t0:kc * NT + t0 + TC],
                       start=(kc == 0), stop=(kc == 15))
                bsl = slice((tc % 2) * TC, (tc % 2) * TC + TC)
                act(sgt[:, bsl], pb[0][:, :TC], AF.Silu)
                tt("ve", hbt[:, bsl], sgt[:, bsl], pb[1][:, :TC], ALU.mult)
                k.dma("sc", HSd[j * P:(j + 1) * P, t0:t0 + TC], hbt[:, bsl])
        k.barrier()
        for q in range(NTC):
            t0 = q * TC
            for kc in range(64):
                k.dma("sy", hnq[:, kc * TC:(kc + 1) * TC],
                      HSd[kc * P:(kc + 1) * P, t0:t0 + TC])
                act(hqt[:, :TC], hnq[:, kc * TC:(kc + 1) * TC], AF.Square)
                mm(pb[4][0:1, :TC], onecb[:, :], hqt[:, :TC],
                   start=(kc == 0), stop=(kc == 63))
            ts("ve", rowB[0:1, :TC], pb[4][0:1, :TC], 1.0 / HF, RMS_EPS,
               ALU.mult, ALU.add)
            act(rowD[0:1, :TC], rowB[0:1, :TC], AF.Ln)
            act(rowD[0:1, :TC], rowD[0:1, :TC], AF.Exp, scale=-0.5)
            mm(pb[2][:, :TC], oner[:, :], rowD[0:1, :TC], start=True, stop=True)
            for kc in range(64):
                tt("ve", hnq[:, kc * TC:(kc + 1) * TC],
                   hnq[:, kc * TC:(kc + 1) * TC], pb[2][:, :TC], ALU.mult)
            for mc in range(16):
                wbb = wb0 if mc % 2 == 0 else wb1
                for piece in range(4):
                    psl = slice(piece * 16 * P, (piece + 1) * 16 * P)
                    k.dma("sy", wbb[:, (piece % 2) * 16 * P:
                                     (piece % 2 + 1) * 16 * P],
                          WPJq[mc][:, psl])
                    for kc in range(16):
                        kk = piece * 16 + kc
                        mm(pb[mc % 2][:, :TC],
                           wbb[:, (piece % 2) * 16 * P + kc * P:
                                  (piece % 2) * 16 * P + (kc + 1) * P],
                           hnq[:, kk * TC:(kk + 1) * TC],
                           start=(kk == 0), stop=(kk == 63))
                k.dma("sy", tp1[:, :TC], xf_old[mc * P:(mc + 1) * P, t0:t0 + TC])
                tt("ve", stg[:, mc * TC:(mc + 1) * TC], pb[mc % 2][:, :TC],
                   tp1[:, :TC], ALU.add)
                if not final:
                    k.dma("sc", xf_new[mc * P:(mc + 1) * P, t0:t0 + TC],
                          stg[:, mc * TC:(mc + 1) * TC])
            if final:
                for mc in range(16):
                    for jb in range(TC // P):
                        transpose_pe(pb[2][:, 0:P],
                                     stg[:, mc * TC + jb * P: mc * TC + (jb + 1) * P])
                        cast("ve", tp0[:, 0:P], pb[2][:, 0:P])
                        k.dma("sc", out_tm[t0 + jb * P: t0 + (jb + 1) * P,
                                           mc * P:(mc + 1) * P], tp0[:, 0:P])
            else:
                ln_tail(q)
        k.barrier()

    def inproj_b():
        load_xln()
        load_wdt(wb_dt)
        for mc in range(64):
            wqb, wbb = (wq0, wb0) if mc % 2 == 0 else (wq1, wb1)
            load_wblock(wb_in[mc], D, wqb, wbb)
            for tc in range(NTC):
                t0 = tc * TC
                po_ = pb[(mc * NTC + tc) % 2]
                for kc in range(16):
                    mm(po_[:, :TC], wbb[:, kc * P:(kc + 1) * P],
                       xln[:, kc * NT + t0:kc * NT + t0 + TC],
                       start=(kc == 0), stop=(kc == 15))
                if mc < 32:
                    cast("ve", cmt[:, :TC], po_[:, :TC])
                    k.dma("sc", BHID[mc * P:(mc + 1) * P, t0:t0 + TC],
                          cmt[:, :TC])
                else:
                    act(tp0[:, :TC], po_[:, :TC], AF.Silu)
                    k.dma("sc", GZ[(mc - 32) * P:(mc - 31) * P, t0:t0 + TC],
                          tp0[:, :TC])
        k.barrier()

    def scan_b():
        conv_mode[0] = "b"
        load_acol(b_al)
        k.dma("sy", selbb[:, :], selB_d[0:32, 0:32 * P])
        for ec in range(32):
            k.dma("sy", qt0[:, :160], wb_x[ec])
            cast("sc", wxb[:, ec * 160:(ec + 1) * 160], qt0[:, :160])
        for tc in range(NTC):
            t0 = tc * TC
            for ec in range(32):
                ust = usta if ec < 16 else ustb
                usl = ust[:, (ec % 16) * TC:((ec % 16) + 1) * TC]
                conv_u(tc, ec, WCVB, bcol_s(4)[:, ec:ec + 1], usl)
                cast("sc", dub[:, :TC], usl)
                mm(pb[2][:, :TC], wxb[:, ec * 160: ec * 160 + P], dub[:, :TC],
                   start=(ec == 0), stop=(ec == 31))
                mm(pb[3][0:32, :TC], wxb[:, ec * 160 + P: ec * 160 + 160],
                   dub[:, :TC], start=(ec == 0), stop=(ec == 31))
            cast("ve", dtl[:, t0:t0 + TC], pb[2][:, :TC])
            cast("ve", cmt[0:32, 0:TC], pb[3][0:32, :TC])
            for n in range(NS):
                mm(pc[0][:, :TC], selbb[:, n * P:(n + 1) * P],
                   cmt[0:32, 0:TC], start=True, stop=True)
                cast("sc", bcb[:, (2 * n) * TC:(2 * n + 1) * TC], pc[0][:, :TC])
                mm(pc[1][:, :TC], selbb[:, (NS + n) * P:(NS + n + 1) * P],
                   cmt[0:32, 0:TC], start=True, stop=True)
                cast("sc", bcb[:, (2 * n + 1) * TC:(2 * n + 2) * TC],
                     pc[1][:, :TC])
            for ec in range(32):
                ust = usta if ec < 16 else ustb
                usl = ust[:, (ec % 16) * TC:((ec % 16) + 1) * TC]
                mm(pb[1][:, :TC], wdt[:, ec * P:(ec + 1) * P], dtl[:, t0:t0 + TC],
                   start=True, stop=True)
                softplus(dts[:, :TC], pb[1][:, :TC], bcol_s(3)[:, ec:ec + 1])
                tt("po", tp2[:, :TC], dts[:, :TC], usl, ALU.mult)
                cast("sc", dub[:, :TC], tp2[:, :TC])
                k.dma("sy", szt[:, :TC], GZ[ec * P:(ec + 1) * P, t0:t0 + TC])
                for n in range(NS):
                    nb = n % 2
                    idx = ec * NS + n
                    act(dab[:, nb * TC:(nb + 1) * TC], dts[:, :TC], AF.Exp,
                        scale=acol[:, idx:idx + 1])
                    tt("ve", dbb[:, nb * TC:(nb + 1) * TC], dub[:, :TC],
                       bcb[:, (2 * n) * TC:(2 * n + 1) * TC], ALU.mult)
                    scan_op(hbb[:, nb * TC:(nb + 1) * TC],
                            dab[:, nb * TC:(nb + 1) * TC],
                            dbb[:, nb * TC:(nb + 1) * TC],
                            0.0 if tc == 0 else carr[:, idx:idx + 1])
                    if tc < NTC - 1:
                        cast("sc", carr[:, idx:idx + 1],
                             hbb[:, nb * TC + TC - 1: nb * TC + TC])
                    if n == 0:
                        tt("ve", yac[:, :TC], hbb[:, nb * TC:(nb + 1) * TC],
                           bcb[:, (2 * n + 1) * TC:(2 * n + 2) * TC], ALU.mult)
                    else:
                        tt("ve", ytb[:, nb * TC:(nb + 1) * TC],
                           hbb[:, nb * TC:(nb + 1) * TC],
                           bcb[:, (2 * n + 1) * TC:(2 * n + 2) * TC], ALU.mult)
                        tt("po", yac[:, :TC], yac[:, :TC],
                           ytb[:, nb * TC:(nb + 1) * TC], ALU.add)
                stt("ve", tp2[:, :TC], usl, bcol_s(5)[:, ec:ec + 1],
                    yac[:, :TC], ALU.mult, ALU.add)
                tt("ve", yfs[:, (ec % 2) * TC:((ec % 2) + 1) * TC], tp2[:, :TC],
                   szt[:, :TC], ALU.mult)
                k.dma("sc", YFd[ec * P:(ec + 1) * P, t0:t0 + TC],
                      yfs[:, (ec % 2) * TC:((ec % 2) + 1) * TC])
        k.barrier()

    def prep_b():
        for name, dst in ((b_bd, bcol_s(3)), (b_cb, bcol_s(4)), (b_Dp, bcol_s(5))):
            k.dma("sy", dst, name)
        load_wdt(wb_dt)
        k.barrier()

    all_phases = [
        ("prep_b", prep_b),
        ("p0", p0),
        ("quant_all", quant_all),
        ("inproj_a", inproj_a),
        ("scan_a", scan_a),
        ("outproj_a", lambda: out_proj(wa_out, XFa, XFb)),
        ("ffn1", lambda: ffn(final=False, xf_old=XFb, xf_new=XFa)),
        ("inproj_b", inproj_b),
        ("scan_b", scan_b),
        ("outproj_b", lambda: out_proj(wb_out, XFa, XFb)),
        ("ffn2", lambda: ffn(final=True, xf_old=XFb, xf_new=None)),
    ]
    for name, fn in all_phases:
        if phases is None or name in phases:
            fn()
    k.barrier()

    k.replay(block)
    es.close()
    return nc, dram


def _host_shared(inputs):
    f32 = np.float32
    import ml_dtypes
    bf16 = ml_dtypes.bfloat16
    g = lambda n: np.asarray(inputs[n])

    W = g("L0_attn_a_W_in")
    wa_in = np.ascontiguousarray(
        W.reshape(73, P, 16, P).transpose(0, 3, 2, 1).reshape(73, P, 16 * P))
    wa_dt = np.ascontiguousarray(g("L0_attn_a_W_dt").T)
    Wo = g("L0_attn_a_W_out")
    wa_out = np.ascontiguousarray(
        Wo.reshape(16, P, 32, P).transpose(0, 3, 2, 1).reshape(16, P, 32 * P))
    cwa = g("L0_attn_a_conv_w")[:, 0, :]
    wa_cvd = np.zeros((32, P, 4 * P), np.float32)
    pi = np.arange(P)
    for ec in range(32):
        for j in range(4):
            wa_cvd[ec, pi, j * P + pi] = cwa[ec * P:(ec + 1) * P, j]
    a_bd = np.ascontiguousarray(g("L0_attn_a_b_dt").reshape(32, P).T)
    a_cb = np.ascontiguousarray(g("L0_attn_a_conv_b").reshape(32, P).T)
    a_al = np.ascontiguousarray(
        g("L0_attn_a_A_log").reshape(32, P, NS).transpose(1, 0, 2).reshape(P, -1))
    a_Dp = np.ascontiguousarray(g("L0_attn_a_D_param").reshape(32, P).T)

    Wb = g("L0_attn_b_W_in")
    wb_in = np.ascontiguousarray(
        Wb.reshape(16, P, 64, P).transpose(2, 1, 0, 3).reshape(64, P, 16 * P))
    wb_x = np.ascontiguousarray(g("L0_attn_b_W_x").reshape(32, P, 160))
    wb_dt = np.ascontiguousarray(g("L0_attn_b_W_dt"))
    Wbo = g("L0_attn_b_W_out")
    wb_out = np.ascontiguousarray(
        Wbo.reshape(32, P, 16, P).transpose(2, 1, 0, 3).reshape(16, P, 32 * P))
    cwb = g("L0_attn_b_conv_kernel")[:, 0, :]
    wb_cvd = np.zeros((32, P, 4 * P), np.float32)
    for ec in range(32):
        for j in range(4):
            wb_cvd[ec, pi, j * P + pi] = cwb[ec * P:(ec + 1) * P, j]
    b_bd = np.ascontiguousarray(g("L0_attn_b_b_dt").reshape(32, P).T)
    b_cb = np.ascontiguousarray(g("L0_attn_b_conv_bias").reshape(32, P).T)
    b_al = np.ascontiguousarray(
        g("L0_attn_b_A_log").reshape(32, P, NS).transpose(1, 0, 2).reshape(P, -1))
    b_Dp = np.ascontiguousarray(g("L0_attn_b_D").reshape(32, P).T)

    Wg = g("L0_ffn_w_gate_up")
    wf_gu = np.ascontiguousarray(
        Wg.reshape(128, P, 16, P).transpose(0, 3, 2, 1).reshape(128, P, 16 * P))
    Wp = g("L0_ffn_w_proj")
    wf_pj = np.ascontiguousarray(
        Wp.reshape(16, P, 64, P).transpose(0, 3, 2, 1).reshape(16, P, 64 * P))

    selB = np.repeat(np.eye(P, dtype=f32)[:, :, None], P, axis=2).reshape(P, -1)
    selC = np.zeros((P, NS, P), dtype=f32)
    for n in range(NS):
        for m in range(P):
            selC[(m // NS) * NS + n, n, m] = 1.0
    selC = selC.reshape(P, -1)
    gsel = np.zeros((P, 2), f32)
    gsel[:64, 0] = 1.0
    gsel[64:, 1] = 1.0
    gselT = np.ascontiguousarray(gsel.T)

    return dict(
        wa_in=wa_in, wa_dt=wa_dt, wa_out=wa_out, wa_cvd=wa_cvd,
        a_bd=a_bd, a_cb=a_cb, a_al=a_al, a_Dp=a_Dp,
        wb_in=wb_in, wb_x=wb_x, wb_dt=wb_dt, wb_out=wb_out, wb_cvd=wb_cvd,
        b_bd=b_bd, b_cb=b_cb, b_al=b_al, b_Dp=b_Dp,
        wf_gu=wf_gu, wf_pj=wf_pj,
        selB=selB.astype(bf16), selC=selC.astype(bf16),
        gsel=gsel.astype(bf16), gselT=gselT.astype(bf16),
        gselTf=gselT.astype(f32),
        ones_c=np.ones((P, 1), f32), ones_cb=np.ones((P, 1), f32).astype(bf16),
        ones_r=np.ones((1, P), f32), identf=np.eye(P, dtype=f32),
    )


def kernel(**inputs):
    nc, dram = build_program(NT=2048)
    shared = _host_shared(inputs)
    x = np.asarray(inputs["L0_attn_a_hidden_states"], dtype=np.float32)
    in_maps = []
    for b in range(8):
        m = dict(shared)
        m["x_tm"] = np.ascontiguousarray(x[b])
        in_maps.append(m)
    res = run_bass_kernel_spmd(nc, in_maps, core_ids=list(range(8)))
    out = np.stack([r["out_tm"].reshape(2048, 2048) for r in res.results], axis=0)
    return out.astype(np.float32)
